# version 08
# baseline (speedup 1.0000x reference)
import os
import numpy as np
import ml_dtypes
import concourse.bass as bass
import concourse.mybir as mybir
from concourse.bass_utils import run_bass_kernel_spmd
from contextlib import ExitStack

F32 = mybir.dt.float32
F32R = mybir.dt.float32r
BF16 = mybir.dt.bfloat16
I32 = mybir.dt.int32
AF = mybir.ActivationFunctionType
OP = mybir.AluOpType
AX = mybir.AxisListType.X

S = 2048
H = 1024
CI = 2048
IF = 2816
NT = 4
T = 512
NTC = 4
NQ = 8
QW = 256
NN = 16
CHH = H // 128
CHI = CI // 128
CHF = IF // 128

C_EPS, C_NH, C_1024, C_2048, C_127, C_R127, C_HALF, C_NEG1, C_ONE, C_N128 = range(10)
CONSTVALS = [1e-5, -0.5, 1.0 / 1024.0, 1.0 / 2048.0, 127.0, 1.0 / 127.0, 0.5, -1.0, 1.0, -128.0]

ENGS = ("sp", "pe", "dve", "act", "pl")


class Sem:
    def __init__(self, h):
        self.h = h
        self.n = 0


class Builder:
    def __init__(self, nc):
        self.nc = nc
        self.em = {e: [] for e in ENGS}

    def on(self, eng, fn):
        self.em[eng].append(fn)

    def wt(self, eng, sem, v=None):
        v = sem.n if v is None else v
        if v <= 0:
            return
        self.on(eng, lambda E, h=sem.h, vv=v: E.wait_ge(h, vv))

    def dma(self, eng, sem, out, in_):
        self.on(eng, lambda E, o=out, i=in_, h=sem.h: E.dma_start(out=o, in_=i).then_inc(h, 16))
        sem.n += 16
        return sem.n

    def op(self, eng, fn, sem=None):
        if sem is None:
            self.on(eng, fn)
            return None
        sem.n += 1
        if eng in ("dve", "act"):
            self.on(eng, fn)
            self.on(eng, lambda E, h=sem.h: E.drain().then_inc(h, 1))
        else:
            self.on(eng, lambda E, f=fn, h=sem.h: f(E).then_inc(h, 1))
        return sem.n


def build_program():
    nc = bass.Bass("TRN2", target_bir_lowering=False, detect_race_conditions=False)
    B = Builder(nc)

    di = {}
    def d_in(name, shape, dt=F32):
        di[name] = nc.dram_tensor(name, shape, dt, kind="ExternalInput").ap()
        return di[name]

    d_x = d_in("x", [H, S])
    d_win = d_in("w_in", [H, 4096])
    d_cw = d_in("conv_w", [128, CHI * 4])
    d_gam = d_in("gamma_t", [128, CHI])
    d_bet = d_in("beta_t", [128, CHI])
    d_drow = d_in("d_row", [1, CI])
    d_wxp = d_in("wxp_t", [128, CHI * 80])
    d_at = d_in("a_t", [16, CI])
    d_wout = d_in("w_out", [CI, H])
    d_cx = d_in("clip_x", [1, 1])
    d_cd = d_in("clip_dn", [1, 1])
    d_wg = d_in("w_gate", [IF, H])
    d_wu = d_in("w_up", [IF, H])
    d_wd = d_in("w_down", [H, IF])
    d_idf = d_in("idf", [128, 128])
    d_idb = d_in("idb", [128, 128], BF16)
    d_ones1 = d_in("ones1", [1, 128])
    d_cst = d_in("consts", [128, len(CONSTVALS)])
    d_y = nc.dram_tensor("y", [H, S], F32, kind="ExternalOutput").ap()

    DBG = bool(os.environ.get("KDBG"))
    dbk = dict(kind="ExternalOutput") if DBG else {}
    t_winb = nc.dram_tensor("winb_d", [H, 4096], BF16).ap()
    t_woutb = nc.dram_tensor("woutb_d", [CI, H], BF16).ap()
    t_abig = nc.dram_tensor("abig_d", [128 * NN * CI], F32).ap()
    t_zsp = nc.dram_tensor("zsp_d", [CI, S], F32).ap()
    t_xmid = nc.dram_tensor("xmid_d", [H, S], F32).ap()
    t_hd = nc.dram_tensor("h_d", [IF, S], F32, **(dict(kind="ExternalOutput") if bool(os.environ.get("KDBG")) else {})).ap()
    t_stash = nc.dram_tensor("stash_d", [2, S], F32).ap()
    t_glhs = nc.dram_tensor("glhs_d", [CHH, 128, IF], BF16, **dbk).ap()
    t_ulhs = nc.dram_tensor("ulhs_d", [CHH, 128, IF], BF16, **dbk).ap()
    t_dlhs = nc.dram_tensor("dlhs_d", [CHF, 128, H], BF16, **dbk).ap()
    t_dbg1 = nc.dram_tensor("dbg1", [CI, S], F32, **dbk).ap()
    t_dbg2 = nc.dram_tensor("dbg2", [CI, S], F32, **dbk).ap()
    t_dbg3 = nc.dram_tensor("dbg3", [8, S], F32, **dbk).ap()
    t_dbg4 = nc.dram_tensor("dbg4", [16, S], F32, **dbk).ap()
    t_dbg5 = nc.dram_tensor("dbg5", [CI, S], F32, **dbk).ap()
    t_dbg6 = nc.dram_tensor("dbg6", [H, S], F32, **dbk).ap()
    t_dbg7 = nc.dram_tensor("dbg7", [128, 16], F32, **dbk).ap()
    cc1_in = nc.dram_tensor("cc1_in", [1, 1], F32).ap()
    cc1_out = nc.dram_tensor("cc1_out", [1, 1], F32).ap()
    cc2_in = nc.dram_tensor("cc2_in", [1, 1], F32).ap()
    cc2_out = nc.dram_tensor("cc2_out", [1, 1], F32).ap()

    es = ExitStack()
    sb = lambda name, shape, dt: es.enter_context(nc.sbuf_tensor(name, shape, dt))

    c_idf = sb("c_idf", [128, 128], F32)
    c_idb = sb("c_idb", [128, 128], BF16)
    c_ones1 = sb("c_ones1", [1, 128], F32)
    c_ones1r = sb("c_ones1r", [1, 128], F32R)
    c_onescolr = sb("c_onescolr", [128, 1], F32R)
    c_cst = sb("c_cst", [128, len(CONSTVALS)], F32)
    cw_sb = sb("cw_sb", [128, CHI * 4], F32)
    gam_sb = sb("gam_sb", [128, CHI], F32)
    bet_sb = sb("bet_sb", [128, CHI], F32)
    wxp_sb = sb("wxp_sb", [128, CHI * 80], F32)
    wxpW = sb("wxpW", [128, CHI * 80], F32R)
    dbc_sb = sb("dbc_sb", [128, CI], BF16)
    rows = sb("rows", [1, 7 * 512], F32)
    rows_r = sb("rows_r", [1, 1280], F32R)
    scal = sb("scal", [128, 16], F32)
    mx1c = sb("mx1c", [128, 32], F32)
    mxp = sb("mxp", [128, 8], F32)
    pw = sb("pw", [128, 1024], F32)
    pt = sb("pt", [128, 1024], F32)
    pi = sb("pi", [128, 1024], I32)
    pf = sb("pf", [128, 1024], F32)
    pg = sb("pg", [128, 1024], F32)
    pq = sb("pq", [128, 1024], BF16)
    pev = sb("pev", [128, 2 * 128], BF16)
    prow = sb("prow", [128, 24], F32)

    pb = [es.enter_context(nc.psum_tensor(f"pb{i}", [128, 512], F32)) for i in range(8)]

    semnames = ["dxa", "dwi", "dwo", "dab", "dzo", "dzi", "dxo", "dxi", "dst",
                "dho", "dhi", "dgl", "dpw", "dy", "dmc", "dwb", "dcc", "ddb",
                "spe", "sdv", "sac", "spl", "sfen", "cc1", "cc2"]
    SM = {n: Sem(es.enter_context(nc.semaphore(n))) for n in semnames}
    g = lambda n: SM[n]
    spe, sdv, sac, spl = g("spe"), g("sdv"), g("sac"), g("spl")

    esB = ExitStack()
    sbB = lambda name, shape, dt: esB.enter_context(nc.sbuf_tensor(name, shape, dt))
    x_pool = sbB("x_pool", [128, 4 * 512], F32)
    ln0bf = sbB("ln0bf", [128, CHH * 512], BF16)
    xr_sl = sbB("xr_sl", [128, 2 * 512], F32R)
    sq_sl = sbB("sq_sl", [128, 2 * 512], F32R)
    winb = sbB("winb", [128, 2 * CHH * 128], BF16)
    xcraw = sbB("xcraw", [128, 3 * 516], BF16)
    tails = sbB("tails", [128, CHI * 4], F32)
    cvd = sbB("cvd", [128, 4 * 128], BF16)
    xcs = sbB("xcs", [128, CHI * 512], F32)
    xcsr = sbB("xcsr", [128, 2 * 512], F32R)
    szsl = sbB("szsl", [128, 2 * 512], F32)
    xpsb = sbB("xpsb", [128, 512], F32)
    usb = None
    dcol = sbB("dcol", [128, 8], F32)
    uT = sbB("uT", [128, 64], F32)
    diag = sbB("diag", [128, 64 * 128], BF16)
    abc = sbB("abc", [128, 4096], F32)
    ebf = sbB("ebf", [128, 4096], BF16)
    stm = sbB("stm", [128, 2 * 256], F32)
    sfm = sbB("sfm", [128, 2 * 512], F32)
    gbf = sbB("gbf", [128, CHI * 512], BF16)
    woutb = sbB("woutb", [128, 2 * CHI * 128], BF16)
    tmp1 = sbB("tmp1", [128, 512], F32)
    usb = tmp1

    sfen = g("sfen")

    def dve_fence():
        sfen.n += 1
        B.on("dve", lambda E, h=sfen.h: E.drain().then_inc(h, 1))
        return sfen.n

    def mm(out, lhsT, rhs, start, stop, sem=None):
        return B.op("pe", lambda E, o=out, l=lhsT, r=rhs, a=start, b=stop:
                    E.matmul(o, l, r, start=a, stop=b), sem)

    def tp(out, in_, ident, sem=None):
        return B.op("pe", lambda E, o=out, i=in_, d=ident: E.transpose(o, i, d), sem)

    def cp(eng, out, in_, sem=None):
        return B.op(eng, lambda E, o=out, i=in_: E.tensor_copy(o, i), sem)

    def tt(out, a, b, o, sem=None, eng="dve"):
        return B.op(eng, lambda E, x=out, p=a, q=b, r=o: E.tensor_tensor(x, p, q, r), sem)

    def ts(out, in_, s1, s2, o1, o2=None, sem=None, eng="dve"):
        if s2 is None:
            return B.op(eng, lambda E, x=out, i=in_, a=s1, p=o1:
                        E.tensor_scalar(x, i, a, None, p), sem)
        return B.op(eng, lambda E, x=out, i=in_, a=s1, b=s2, p=o1, q=o2:
                    E.tensor_scalar(x, i, a, b, p, q), sem)

    def stt(out, in0, s, in1, o1, o2, sem=None, eng="dve"):
        return B.op(eng, lambda E, x=out, i=in0, a=s, j=in1, p=o1, q=o2:
                    E.scalar_tensor_tensor(x, i, a, j, p, q), sem)

    def act(out, in_, fn, bias=0.0, scale=1.0, accum=None, sem=None):
        return B.op("act", lambda E, o=out, i=in_, f=fn, b=bias, s=scale, a=accum:
                    E.activation(o, i, f, bias=b, scale=s, accum_out=a), sem)

    def red(out, in_, o, absval=False, sem=None):
        return B.op("dve", lambda E, x=out, i=in_, p=o, a=absval:
                    E.tensor_reduce(x, i, AX, p, apply_absolute_value=a), sem)

    def mset(eng, ap_, v, sem=None):
        return B.op(eng, lambda E, a=ap_, vv=v: E.memset(a, vv), sem)

    def cc(kind, op_, ins, outs, sem):
        sem.n += 1
        B.on("pl", lambda E, k=kind, o=op_, i=ins, u=outs, h=sem.h:
             E.collective_compute(k, o, replica_groups=[list(range(8))],
                                  ins=i, outs=u).then_inc(h, 1))
        return sem.n

    bank_free = [None] * 8

    def bank_acquire(i, eng):
        if bank_free[i] is not None:
            s, v = bank_free[i]
            B.wt(eng, s, v)
            bank_free[i] = None

    def bank_release(i, sem):
        bank_free[i] = (sem, sem.n)

    rowap = lambda i: rows[0:1, 512 * i:512 * i + 512]
    row1 = lambda i, w=1: rows[0:1, 512 * i:512 * i + w]
    cst = lambda c: c_cst[0:1, c:c + 1]
    cstp = lambda c: c_cst[:, c:c + 1]

    dmc, dwb = g("dmc"), g("dwb")
    for out, src in [(c_idf[:], d_idf), (c_idb[:], d_idb), (c_ones1[:], d_ones1),
                     (c_cst[:], d_cst), (cw_sb[:], d_cw), (gam_sb[:], d_gam),
                     (bet_sb[:], d_bet), (wxp_sb[:], d_wxp)]:
        B.dma("sp", dmc, out, src)
    B.dma("sp", dmc, scal[0:1, 0:1], d_cx)
    B.dma("sp", dmc, scal[0:1, 1:2], d_cd)
    B.dma("sp", dmc, xcs[0:16, 0:CI], d_at)

    n_dmc0 = dmc.n
    B.wt("pe", dmc, n_dmc0)
    B.wt("act", dmc, n_dmc0)
    act(scal[0:1, 2:3], scal[0:1, 0:1], AF.Sigmoid)
    act(scal[0:1, 3:4], scal[0:1, 1:2], AF.Sigmoid, sem=sac)

    B.wt("dve", dmc, n_dmc0)
    cp("dve", c_ones1r[:], c_ones1[:])
    cp("dve", c_onescolr[:], c_cst[:, C_ONE:C_ONE + 1])
    mset("dve", rowap(6), 1.0, sem=sdv)
    B.wt("dve", sdv)
    cp("dve", rows_r[0:1, 0:512], rowap(6), sem=sdv)
    n_ones_r = sdv.n

    at = xcs[0:16, 0:CI]
    a_abs = xcs[32:48, 0:CI]
    a_rel = xcs[32:48, CI:2 * CI]
    B.wt("act", sdv, n_ones_r)
    act(a_abs, at, AF.Abs, sem=sac)
    B.wt("act", sac)
    act(a_abs, a_abs, AF.Exp, scale=c_cst[0:16, C_NEG1:C_NEG1 + 1], sem=sac)
    B.wt("act", sac)
    act(a_abs, a_abs, AF.Ln, bias=c_cst[0:16, C_ONE:C_ONE + 1], sem=sac)
    B.wt("act", sac)
    act(a_rel, at, AF.Relu, sem=sac)
    n_aact = sac.n
    B.wt("dve", sac, n_aact)
    tt(a_abs, a_abs, a_rel, OP.add)
    ts(a_abs, a_abs, 0.1, 10.0, OP.max, OP.min)
    ts(a_abs, a_abs, -1.0, None, OP.mult, sem=sdv)
    n_aprime = sdv.n
    B.wt("sp", sdv, n_aprime)
    B.dma("sp", dwb, t_abig[0:NN * CI].rearrange("(a b) -> a b", a=16), a_abs)
    nsz = NN * CI
    for _ in range(7):
        B.wt("sp", dwb)
        B.dma("sp", dwb, t_abig[nsz:2 * nsz], t_abig[0:nsz])
        nsz *= 2
    B.wt("sp", dwb)
    n_abig = dwb.n

    B.dma("sp", dmc, xcs[96:97, 0:CI], d_drow)
    n_drow = dmc.n
    B.wt("dve", dmc, n_drow)
    cp("dve", ln0bf[0:1, 0:CI], xcs[96:97, 0:CI], sem=sdv)
    n_drowb = sdv.n
    B.wt("dve", dmc, n_dmc0)
    cp("dve", gbf[0:1, 0:128], c_ones1[:], sem=sdv)
    n_ones1b = sdv.n
    B.wt("pe", sdv, max(n_drowb, n_ones1b))
    for kb in range(4):
        bank_acquire(kb, "pe")
    mm(pb[0][:, 0:512], gbf[0:1, 0:128], ln0bf[0:1, 0:512], True, True)
    mm(pb[1][:, 0:512], gbf[0:1, 0:128], ln0bf[0:1, 512:1024], True, True)
    mm(pb[2][:, 0:512], gbf[0:1, 0:128], ln0bf[0:1, 1024:1536], True, True)
    mm(pb[3][:, 0:512], gbf[0:1, 0:128], ln0bf[0:1, 1536:2048], True, True, sem=spe)
    n_dbc = spe.n
    B.wt("dve", spe, n_dbc)
    for k in range(4):
        cp("dve", dbc_sb[:, 512 * k:512 * k + 512], pb[k][:, 0:512])
    B.op("dve", lambda E: E.engine_nop(), sem=sdv)
    n_dbcs = sdv.n
    for k in range(4):
        bank_release(k, sdv)

    B.wt("dve", dmc, n_dmc0)
    for c in range(CHI):
        ts(wxpW[:, 80 * c:80 * c + 80], wxp_sb[:, 80 * c:80 * c + 80],
           gam_sb[:, c:c + 1], None, OP.mult, sem=sdv if c == CHI - 1 else None)
    n_wxpw = sdv.n
    B.wt("pe", sdv, n_wxpw)
    bank_acquire(4, "pe"); bank_acquire(5, "pe")
    for c in range(CHI):
        mm(pb[4][0:80, 0:1], wxp_sb[:, 80 * c:80 * c + 80], gam_sb[:, c:c + 1], c == 0, c == CHI - 1)
    for c in range(CHI):
        mm(pb[5][0:80, 0:1], wxp_sb[:, 80 * c:80 * c + 80], bet_sb[:, c:c + 1],
           c == 0, c == CHI - 1, sem=spe if c == CHI - 1 else None)
    n_cgb = spe.n
    B.wt("dve", spe, n_cgb)
    cp("dve", xpsb[0:80, 0:1], pb[4][0:80, 0:1])
    cp("dve", xpsb[0:80, 1:2], pb[5][0:80, 0:1], sem=sdv)
    n_cgev = sdv.n
    bank_release(4, sdv); bank_release(5, sdv)
    B.wt("pe", sdv, n_cgev)
    bank_acquire(7, "pe")
    tp(pb[7][0:1, 0:80], xpsb[0:80, 0:1], c_idf[0:80, 0:80], sem=spe)
    n_cgt = spe.n
    B.wt("dve", spe, n_cgt)
    cp("dve", rows_r[0:1, 512:592], pb[7][0:1, 0:80])
    cp("dve", scal[0:80, 8:9], xpsb[0:80, 1:2], sem=sdv)
    n_cgr = sdv.n
    bank_release(7, sdv)
    cg_r = rows_r[0:1, 512:592]
    cb_col = scal[0:80, 8:9]
    negmur_r = rows_r[0:1, 672:1184]
    ones512_r = rows_r[0:1, 0:512]

    B.wt("sp", sac)
    B.wt("sp", spe)
    B.wt("dve", spe)
    for k in range(CHH):
        B.wt("sp", sdv)
        B.wt("sp", dwb)
        B.dma("sp", dmc, xcs[:, 0:4096], d_win[128 * k:128 * k + 128, :])
        B.wt("dve", dmc)
        B.wt("dve", dwb)
        cp("dve", ln0bf[:, 0:4096], xcs[:, 0:4096], sem=sdv)
        B.wt("sp", sdv)
        B.dma("sp", dwb, t_winb[128 * k:128 * k + 128, :], ln0bf[:, 0:4096])
    for k in range(CHI):
        B.wt("sp", sdv)
        B.wt("sp", dwb)
        B.dma("sp", dmc, xcs[:, 4096:4096 + 1024], d_wout[128 * k:128 * k + 128, :])
        B.wt("dve", dmc)
        B.wt("dve", dwb)
        cp("dve", ln0bf[:, 0:1024], xcs[:, 4096:4096 + 1024], sem=sdv)
        B.wt("sp", sdv)
        B.dma("sp", dwb, t_woutb[128 * k:128 * k + 128, :], ln0bf[:, 0:1024])
    n_wcast = dwb.n

    mset("dve", tails[:], 0.0)
    mset("dve", mx1c[:], 0.0)
    mset("dve", mxp[:], 1.0, sem=sdv)
    n_zeros = sdv.n

    dpw, dgl = g("dpw"), g("dgl")

    def quant_rows(src_ap, ncols):
        mset("dve", prow[:, 0:8], 1.0, sem=sdv)
        B.wt("dve", sdv)
        red(prow[:, 0:1], src_ap, OP.max, absval=True, sem=sdv)
        B.wt("dve", sdv)
        B.op("dve", lambda E: E.reciprocal(prow[:, 8:16], prow[:, 0:8]), sem=sdv)
        B.wt("dve", sdv)
        ts(prow[:, 16:17], prow[:, 8:9], cstp(C_127), None, OP.mult)
        ts(prow[:, 17:18], prow[:, 0:1], cstp(C_R127), None, OP.mult)
        ts(pt[:, 0:ncols], src_ap, prow[:, 16:17], cstp(C_HALF), OP.mult, OP.add)
        cp("dve", pi[:, 0:ncols], pt[:, 0:ncols])
        cp("dve", pf[:, 0:ncols], pi[:, 0:ncols])
        tt(pg[:, 0:ncols], pf[:, 0:ncols], pt[:, 0:ncols], OP.is_gt)
        tt(pf[:, 0:ncols], pf[:, 0:ncols], pg[:, 0:ncols], OP.subtract)
        ts(pq[:, 0:ncols], pf[:, 0:ncols], prow[:, 17:18], None, OP.mult, sem=sdv)
        dve_fence()

    def prep_weight(w_ap, nrows_chunks, ncols, out_dram, transpose_blocks):
        for c in range(nrows_chunks):
            B.wt("pl", sdv)
            B.dma("pl", dpw, pw[:, 0:ncols], w_ap[128 * c:128 * c + 128, 0:ncols])
            B.wt("dve", dpw)
            B.wt("dve", spe)
            quant_rows(pw[:, 0:ncols], ncols)
            nblk = ncols // 128
            for j in range(nblk):
                bidx = 6 + (j % 2)
                bank_acquire(bidx, "pe")
                B.wt("pe", sdv)
                B.wt("pe", sfen)
                tp(pb[bidx].bitcast(BF16)[:, 0:128], pq[:, 128 * j:128 * j + 128],
                   c_idb[:], sem=spe)
                B.wt("dve", spe)
                B.wt("dve", dgl)
                cp("dve", pev[:, 128 * (j % 2):128 * (j % 2) + 128],
                   pb[bidx].bitcast(BF16)[:, 0:128], sem=sdv)
                bank_release(bidx, sdv)
                B.wt("pl", sdv)
                B.dma("pl", dgl, out_dram(c, j),
                      pev[:, 128 * (j % 2):128 * (j % 2) + 128])
        B.wt("pl", dgl)

    prep_weight(d_wg, CHF, H, lambda c, j: t_glhs[j, :, 128 * c:128 * c + 128], True)
    prep_weight(d_wu, CHF, H, lambda c, j: t_ulhs[j, :, 128 * c:128 * c + 128], True)

    for c in range(CHH):
        mset("dve", prow[:, 0:8], 1.0)
        mset("dve", prow[:, 18:19], 0.0, sem=sdv)
        for s3, (off, w3) in enumerate([(0, 1024), (1024, 1024), (2048, 768)]):
            B.wt("pl", sdv)
            B.dma("pl", dpw, pw[:, 0:w3], d_wd[128 * c:128 * c + 128, off:off + w3])
            B.wt("dve", dpw)
            red(prow[:, 1:2], pw[:, 0:w3], OP.max, absval=True, sem=sdv)
            B.wt("dve", sdv)
            tt(prow[:, 18:19], prow[:, 18:19], prow[:, 1:2], OP.max, sem=sdv)
        B.wt("dve", sdv)
        cp("dve", prow[:, 0:1], prow[:, 18:19], sem=sdv)
        B.wt("dve", sdv)
        B.op("dve", lambda E: E.reciprocal(prow[:, 8:16], prow[:, 0:8]), sem=sdv)
        B.wt("dve", sdv)
        ts(prow[:, 16:17], prow[:, 8:9], cstp(C_127), None, OP.mult)
        ts(prow[:, 17:18], prow[:, 0:1], cstp(C_R127), None, OP.mult, sem=sdv)
        for s3, (off, w3) in enumerate([(0, 1024), (1024, 1024), (2048, 768)]):
            B.wt("pl", sdv)
            B.dma("pl", dpw, pw[:, 0:w3], d_wd[128 * c:128 * c + 128, off:off + w3])
            B.wt("dve", dpw)
            B.wt("dve", spe)
            ts(pt[:, 0:w3], pw[:, 0:w3], prow[:, 16:17], cstp(C_HALF), OP.mult, OP.add)
            cp("dve", pi[:, 0:w3], pt[:, 0:w3])
            cp("dve", pf[:, 0:w3], pi[:, 0:w3])
            tt(pg[:, 0:w3], pf[:, 0:w3], pt[:, 0:w3], OP.is_gt)
            tt(pf[:, 0:w3], pf[:, 0:w3], pg[:, 0:w3], OP.subtract)
            ts(pq[:, 0:w3], pf[:, 0:w3], prow[:, 17:18], None, OP.mult, sem=sdv)
            dve_fence()
            for j in range(w3 // 128):
                jj = off // 128 + j
                bidx = 6 + (jj % 2)
                bank_acquire(bidx, "pe")
                B.wt("pe", sdv)
                B.wt("pe", sfen)
                tp(pb[bidx].bitcast(BF16)[:, 0:128], pq[:, 128 * j:128 * j + 128],
                   c_idb[:], sem=spe)
                B.wt("dve", spe)
                B.wt("dve", dgl)
                cp("dve", pev[:, 128 * (jj % 2):128 * (jj % 2) + 128],
                   pb[bidx].bitcast(BF16)[:, 0:128], sem=sdv)
                bank_release(bidx, sdv)
                B.wt("pl", sdv)
                B.dma("pl", dgl, t_dlhs[jj, :, 128 * c:128 * c + 128],
                      pev[:, 128 * (jj % 2):128 * (jj % 2) + 128])
    B.wt("pl", dgl)
    n_prep_done = dgl.n

    dxa, dwi, dwo, dab = g("dxa"), g("dwi"), g("dwo"), g("dab")
    dzo, dzi, dxo, dxi, dst = g("dzo"), g("dzi"), g("dxo"), g("dxi"), g("dst")

    xslot_free = [[] for _ in range(4)]
    xload_k = [0]

    def load_x(rows_lo, cols_lo, src=None, dsem=None):
        k = xload_k[0] % 4
        for s, v in xslot_free[k]:
            B.wt("sp", s, v)
        xslot_free[k] = []
        src = d_x if src is None else src
        dsem = dxa if dsem is None else dsem
        B.dma("sp", dsem, x_pool[:, 512 * k:512 * k + 512],
              src[rows_lo:rows_lo + 128, cols_lo:cols_lo + 512])
        xload_k[0] += 1
        return k, dsem.n

    win_free = [None, None]
    wout_free = [None, None]
    abc_free = [None, None]
    xcraw_free = [None] * 3
    xcsr_free = [None, None]
    szsl_free = [None, None]
    zsp_written = {}

    for e in ENGS:
        B.wt(e, dwb, dwb.n)
        B.wt(e, g("dmc"), g("dmc").n)
    B.wt("dve", sdv, n_zeros)

    def ln_stats_chunk(src_f32, slot2, bsum, bssq, first, last):
        xr = xr_sl[:, 512 * slot2:512 * slot2 + 512]
        sq = sq_sl[:, 512 * slot2:512 * slot2 + 512]
        cp("dve", xr, src_f32, sem=sdv)
        tt(sq, xr, xr, OP.mult, sem=sdv)
        dve_fence()
        B.wt("pe", sdv)
        B.wt("pe", sfen)
        mm(pb[bsum][0:1, 0:512], c_onescolr[:], xr, first, last)
        mm(pb[bssq][0:1, 0:512], c_onescolr[:], sq, first, last, sem=spe)

    def ln_rows(bsum, bssq, mu_i, rstd_i, invn_c, tmp_i):
        B.wt("dve", spe)
        B.wt("dve", g("dst"))
        ts(rowap(mu_i), pb[bsum][0:1, 0:512], cst(invn_c), None, OP.mult, sem=sdv)
        ts(rowap(tmp_i), pb[bssq][0:1, 0:512], cst(invn_c), None, OP.mult, sem=sdv)
        bank_release(bsum, sdv); bank_release(bssq, sdv)
        tt(rowap(rstd_i), rowap(mu_i), rowap(mu_i), OP.mult)
        tt(rowap(rstd_i), rowap(tmp_i), rowap(rstd_i), OP.subtract, sem=sdv)
        B.wt("act", sdv)
        act(rowap(rstd_i), rowap(rstd_i), AF.Ln, bias=cst(C_EPS), sem=sac)
        B.wt("act", sac)
        act(rowap(rstd_i), rowap(rstd_i), AF.Exp, scale=cst(C_NH), sem=sac)

    def bcast_rows(row_ap, bidx, sem=None, val=None):
        sem = sac if sem is None else sem
        bank_acquire(bidx, "pe")
        B.wt("pe", sem, sem.n if val is None else val)
        mm(pb[bidx][:, 0:512], c_ones1[:], row_ap, True, True, sem=spe)

    for t in range(NT):
        tc0 = t * T
        bank_acquire(0, "pe"); bank_acquire(1, "pe")
        for c in range(CHH):
            k, dn = load_x(128 * c, tc0)
            B.wt("dve", dxa, dn)
            if xcsr_free[c % 2] is not None:
                s, v = xcsr_free[c % 2]
            B.wt("dve", spe)
            ln_stats_chunk(x_pool[:, 512 * k:512 * k + 512], c % 2, 0, 1,
                           c == 0, c == CHH - 1)
            xslot_free[k] = [(sdv, sdv.n)]
        ln_rows(0, 1, mu_i=0, rstd_i=2, invn_c=C_1024, tmp_i=1)
        bcast_rows(rowap(0), 2)
        bcast_rows(rowap(2), 3)
        n_bc0 = spe.n
        for c in range(CHH):
            k, dn = load_x(128 * c, tc0)
            B.wt("dve", dxa, dn)
            B.wt("dve", spe, n_bc0)
            tt(tmp1[:], x_pool[:, 512 * k:512 * k + 512], pb[2][:, 0:512], OP.subtract)
            tt(ln0bf[:, 512 * c:512 * c + 512], tmp1[:], pb[3][:, 0:512], OP.mult, sem=sdv)
            xslot_free[k] = [(sdv, sdv.n)]
        n_ln0 = dve_fence()
        bank_release(2, sdv); bank_release(3, sdv)

        bank_acquire(0, "pe"); bank_acquire(1, "pe")
        B.wt("sp", dwb, n_wcast)
        for m in range(32):
            sl = m % 2
            if win_free[sl] is not None:
                s, v = win_free[sl]
                B.wt("sp", s, v)
            B.dma("sp", dwi, winb[:, 1024 * sl:1024 * sl + 1024],
                  t_winb.rearrange("(k p) m -> p k m", p=128)[:, :, 128 * m:128 * m + 128])
            bm = 4 + (m % 2)
            bank_acquire(bm, "pe")
            B.wt("pe", dwi)
            B.wt("pe", sfen, n_ln0)
            for kk in range(CHH):
                mm(pb[bm][:, 0:512], winb[:, 1024 * sl + 128 * kk:1024 * sl + 128 * kk + 128],
                   ln0bf[:, 512 * kk:512 * kk + 512], kk == 0, kk == CHH - 1,
                   sem=spe if kk == CHH - 1 else None)
            win_free[sl] = (spe, spe.n)
            if m < CHI:
                c = m
                rsl = c % 3
                if xcraw_free[rsl] is not None:
                    s, v = xcraw_free[rsl]
                    B.wt("dve", s, v)
                B.wt("dve", spe)
                xcr = xcraw[:, 516 * rsl:516 * rsl + 516]
                cp("dve", xcr[:, 0:3], tails[:, 4 * c:4 * c + 3], sem=sdv)
                cp("dve", xcr[:, 3:515], pb[bm][:, 0:512], sem=sdv)
                B.wt("dve", sdv)
                cp("dve", tails[:, 4 * c:4 * c + 3], xcr[:, 512:515], sem=sdv)
                for kk in range(4):
                    ts(cvd[:, 128 * kk:128 * kk + 128], c_idb[:],
                       cw_sb[:, 4 * c + kk:4 * c + kk + 1], None, OP.mult,
                       sem=sdv if kk == 3 else None)
                dve_fence()
                bank_release(bm, sdv)
                bcv = 2 + (c % 2)
                bank_acquire(bcv, "pe")
                B.wt("pe", sdv)
                B.wt("pe", sfen)
                for kk in range(4):
                    mm(pb[bcv][:, 0:512], cvd[:, 128 * kk:128 * kk + 128],
                       xcr[:, kk:kk + 512], kk == 0, kk == 3,
                       sem=spe if kk == 3 else None)
                xcraw_free[rsl] = (spe, spe.n)
                B.wt("act", spe)
                act(xcs[:, 512 * c:512 * c + 512], pb[bcv][:, 0:512], AF.Silu, sem=sac)
                bank_release(bcv, sac)
                B.wt("dve", sac)
                if xcsr_free[c % 2] is not None:
                    s, v = xcsr_free[c % 2]
                    B.wt("dve", s, v)
                B.wt("dve", spe)
                xrr = xcsr[:, 512 * (c % 2):512 * (c % 2) + 512]
                cp("dve", xrr, xcs[:, 512 * c:512 * c + 512], sem=sdv)
                sq = sq_sl[:, 512 * (c % 2):512 * (c % 2) + 512]
                tt(sq, xrr, xrr, OP.mult, sem=sdv)
                dve_fence()
                B.wt("pe", sdv)
                B.wt("pe", sfen)
                mm(pb[0][0:1, 0:512], c_onescolr[:], xrr, c == 0, c == CHI - 1)
                mm(pb[1][0:1, 0:512], c_onescolr[:], sq, c == 0, c == CHI - 1)
                if c == 0:
                    bank_acquire(7, "pe")
                mm(pb[7][0:80, 0:512], wxpW[:, 80 * c:80 * c + 80], xrr,
                   c == 0, False, sem=spe)
                xcsr_free[c % 2] = (spe, spe.n)
            else:
                czz = m - CHI
                zsl = czz % 2
                if szsl_free[zsl] is not None:
                    s, v = szsl_free[zsl]
                    B.wt("act", s, v)
                B.wt("act", spe)
                act(szsl[:, 512 * zsl:512 * zsl + 512], pb[bm][:, 0:512], AF.Silu, sem=sac)
                bank_release(bm, sac)
                B.wt("pl", sac)
                B.dma("pl", dzo, t_zsp[128 * czz:128 * czz + 128, tc0:tc0 + 512],
                      szsl[:, 512 * zsl:512 * zsl + 512])
                szsl_free[zsl] = (dzo, dzo.n)
                zsp_written[(t, czz)] = dzo.n
        ln_rows(0, 1, mu_i=0, rstd_i=2, invn_c=C_2048, tmp_i=1)
        B.wt("dve", sac)
        ts(rowap(4), rowap(0), -1.0, None, OP.mult)
        cp("dve", negmur_r, rowap(4), sem=sdv)
        n_nmr = sdv.n
        B.wt("pe", sdv, n_nmr)
        mm(pb[7][0:80, 0:512], cg_r, negmur_r, False, True, sem=spe)
        n_xp = spe.n
        bcast_rows(rowap(0), 2)
        bcast_rows(rowap(2), 3)
        n_bc2 = spe.n
        B.wt("dve", spe, n_xp)
        cp("dve", xpsb[0:80, 0:512], pb[7][0:80, 0:512])
        B.wt("dve", spe, n_bc2)
        tt(xpsb[0:80, 0:512], xpsb[0:80, 0:512], pb[3][0:80, 0:512], OP.mult)
        ts(xpsb[0:80, 0:512], xpsb[0:80, 0:512], cb_col, None, OP.add, sem=sdv)
        bank_release(7, sdv)
        cp("dve", sfm[0:16, 0:512], xpsb[32:48, 0:512])
        cp("dve", sfm[0:16, 512:1024], xpsb[64:80, 0:512])
        tt(usb[0:16, 0:512], sfm[0:16, 0:512], sfm[0:16, 512:1024], OP.mult, sem=sdv)
        n_u = sdv.n
        B.wt("act", sdv, n_u)
        act(rowap(5), xpsb[0:1, 0:512], AF.Abs, sem=sac)
        B.wt("act", sac)
        act(rowap(5), rowap(5), AF.Exp, scale=cst(C_NEG1), sem=sac)
        B.wt("act", sac)
        act(rowap(5), rowap(5), AF.Ln, bias=cst(C_ONE), sem=sac)
        B.wt("act", sac)
        act(rowap(1), xpsb[0:1, 0:512], AF.Relu, sem=sac)
        B.wt("dve", sac)
        tt(rowap(5), rowap(5), rowap(1), OP.add)
        ts(rowap(5), rowap(5), 1e-6, 10.0, OP.max, OP.min, sem=sdv)
        n_delta = sdv.n
        bank_acquire(4, "pe")
        B.wt("pe", sdv, n_delta)
        for tcg in range(NTC):
            tp(pb[4][:, tcg:tcg + 1], rowap(5)[0:1, 128 * tcg:128 * tcg + 128],
               c_idf[0:1, 0:1])
        for tcg in range(NTC):
            tp(pb[4][:, 16 + 16 * tcg:16 + 16 * tcg + 16],
               usb[0:16, 128 * tcg:128 * tcg + 128], c_idf[0:16, 0:16],
               sem=spe if tcg == NTC - 1 else None)
        n_tr = spe.n
        B.wt("dve", spe, n_tr)
        cp("dve", dcol[:, 0:4], pb[4][:, 0:4], sem=sdv)
        cp("dve", uT[:, 0:64], pb[4][:, 16:80], sem=sdv)
        bank_release(4, sdv)
        for tcg in range(NTC):
            for n in range(NN):
                idx = tcg * NN + n
                ts(diag[:, 128 * idx:128 * idx + 128], c_idb[:],
                   uT[:, idx:idx + 1], None, OP.mult,
                   sem=sdv if idx == NTC * NN - 1 else None)
        n_diag = dve_fence()
        if DBG:
            B.wt("pl", sdv, n_diag)
            for c in range(CHI):
                B.dma("pl", g("ddb"), t_dbg1[128 * c:128 * c + 128, tc0:tc0 + 512],
                      xcs[:, 512 * c:512 * c + 512])
            B.dma("pl", g("ddb"), t_dbg3[0:1, tc0:tc0 + 512], rowap(5))
            B.dma("pl", g("ddb"), t_dbg4[0:16, tc0:tc0 + 512], usb[0:16, 0:512])
            B.wt("pl", g("ddb"))

        for q in range(NQ):
            asl = 0
            if abc_free[asl] is not None:
                s, v = abc_free[asl]
                B.wt("sp", s, v)
            B.wt("sp", dwb, n_abig)
            src = t_abig.rearrange("(p n i) -> p n i", p=128, n=NN)[:, :, QW * q:QW * q + QW]
            B.dma("sp", dab, abc[:, 4096 * asl:4096 * asl + 4096].rearrange("p (n i) -> p n i", n=NN), src)
            dab_n = g("dab").n
            for tcg in range(NTC):
                B.wt("act", g("dab"), dab_n)
                B.wt("act", sfen, n_diag)
                B.wt("act", spe)
                act(ebf[:], abc[:, 4096 * asl:4096 * asl + 4096], AF.Exp,
                    scale=dcol[:, tcg:tcg + 1], sem=sac)
                B.wt("dve", sac)
                ts(ebf[:], ebf[:], 1e-6, 1.0, OP.max, OP.min, sem=sdv)
                dve_fence()
                bS = 4 + (tcg % 2)
                bank_acquire(bS, "pe")
                B.wt("pe", sdv)
                B.wt("pe", sfen)
                for n in range(NN):
                    idx = tcg * NN + n
                    mm(pb[bS][:, 0:QW], diag[:, 128 * idx:128 * idx + 128],
                       ebf[:, QW * n:QW * n + QW], n == 0, False)
                mm(pb[bS][:, 0:QW], c_idb[:], dbc_sb[:, QW * q:QW * q + QW],
                   False, True, sem=spe)
                if tcg == NTC - 1:
                    abc_free[asl] = (spe, spe.n)
                B.wt("dve", spe)
                ssl = tcg % 2
                cp("dve", stm[:, 256 * ssl:256 * ssl + 256], pb[bS][:, 0:QW], sem=sdv)
                bank_release(bS, sdv)
                bT = 6
                bank_acquire(bT, "pe")
                B.wt("pe", sdv)
                tp(pb[bT][:, 0:128], stm[:, 256 * ssl:256 * ssl + 128], c_idf[:])
                tp(pb[bT][:, 128:256], stm[:, 256 * ssl + 128:256 * ssl + 256],
                   c_idf[:], sem=spe)
                B.wt("dve", spe)
                cp("dve", sfm[:, 128 * tcg:128 * tcg + 128], pb[bT][:, 0:128])
                cp("dve", sfm[:, 512 + 128 * tcg:512 + 128 * tcg + 128],
                   pb[bT][:, 128:256], sem=sdv)
                bank_release(bT, sdv)
            for half in range(2):
                c = 2 * q + half
                zsl = c % 2
                if szsl_free[zsl] is not None:
                    s, v = szsl_free[zsl]
                    B.wt("sp", s, v)
                B.wt("sp", dzo, zsp_written[(t, c)])
                B.dma("sp", dzi, szsl[:, 512 * zsl:512 * zsl + 512],
                      t_zsp[128 * c:128 * c + 128, tc0:tc0 + 512])
                B.wt("dve", dzi)
                B.wt("dve", spe)
                tt(tmp1[:], xcs[:, 512 * c:512 * c + 512], pb[2][:, 0:512], OP.subtract)
                tt(tmp1[:], tmp1[:], pb[3][:, 0:512], OP.mult)
                ts(tmp1[:], tmp1[:], gam_sb[:, c:c + 1], bet_sb[:, c:c + 1],
                   OP.mult, OP.add)
                tt(tmp1[:], tmp1[:], sfm[:, 512 * half:512 * half + 512], OP.mult)
                tt(gbf[:, 512 * c:512 * c + 512], tmp1[:],
                   szsl[:, 512 * zsl:512 * zsl + 512], OP.mult, sem=sdv)
                szsl_free[zsl] = (sdv, sdv.n)
                dve_fence()
                if DBG:
                    B.wt("pl", sdv)
                    B.dma("pl", g("ddb"), t_dbg2[128 * c:128 * c + 128, tc0:tc0 + 512],
                          sfm[:, 512 * half:512 * half + 512])
                    B.wt("pl", g("ddb"))
        n_g = dve_fence()
        bank_release(2, sdv); bank_release(3, sdv)

        bank_acquire(0, "pe"); bank_acquire(1, "pe")
        B.wt("sp", dwb, n_wcast)
        for m in range(CHH):
            sl = m % 2
            if wout_free[sl] is not None:
                s, v = wout_free[sl]
                B.wt("sp", s, v)
            B.dma("sp", dwo, woutb[:, 2048 * sl:2048 * sl + 2048],
                  t_woutb.rearrange("(k p) m -> p k m", p=128)[:, :, 128 * m:128 * m + 128])
            bm = 4 + (m % 2)
            bank_acquire(bm, "pe")
            B.wt("pe", dwo)
            B.wt("pe", sfen, n_g)
            for kk in range(CHI):
                mm(pb[bm][:, 0:512],
                   woutb[:, 2048 * sl + 128 * kk:2048 * sl + 128 * kk + 128],
                   gbf[:, 512 * kk:512 * kk + 512], kk == 0, kk == CHI - 1,
                   sem=spe if kk == CHI - 1 else None)
            wout_free[sl] = (spe, spe.n)
            k, dn = load_x(128 * m, tc0)
            B.wt("dve", dxa, dn)
            B.wt("dve", spe)
            tt(x_pool[:, 512 * k:512 * k + 512], pb[bm][:, 0:512],
               x_pool[:, 512 * k:512 * k + 512], OP.add, sem=sdv)
            bank_release(bm, sdv)
            B.wt("sp", sdv)
            B.dma("sp", dxo, t_xmid[128 * m:128 * m + 128, tc0:tc0 + 512],
                  x_pool[:, 512 * k:512 * k + 512])
            B.wt("dve", spe)
            ln_stats_chunk(x_pool[:, 512 * k:512 * k + 512], m % 2, 0, 1,
                           m == 0, m == CHH - 1)
            xslot_free[k] = [(dxo, dxo.n), (sdv, sdv.n)]
        ln_rows(0, 1, mu_i=0, rstd_i=2, invn_c=C_1024, tmp_i=1)
        B.wt("dve", sac)
        tt(rowap(4), rowap(0), rowap(2), OP.mult)
        ts(rowap(4), rowap(4), -1.0, None, OP.mult, sem=sdv)
        B.wt("sp", sdv)
        B.dma("sp", dst, t_stash[0:1, tc0:tc0 + 512], rowap(2))
        B.dma("sp", dst, t_stash[1:2, tc0:tc0 + 512], rowap(4))
        bcast_rows(rowap(2), 4)
        bcast_rows(rowap(4), 5)
        n_bc1 = spe.n
        for m in range(CHH):
            B.wt("sp", dxo)
            k, dn = load_x(128 * m, tc0, src=t_xmid, dsem=dxi)
            if DBG:
                B.wt("pl", dxi, dn)
                B.dma("pl", g("ddb"), t_dbg6[128 * m:128 * m + 128, tc0:tc0 + 512],
                      x_pool[:, 512 * k:512 * k + 512])
            B.wt("dve", dxi, dn)
            B.wt("dve", spe, n_bc1)
            tt(tmp1[:], x_pool[:, 512 * k:512 * k + 512], pb[4][:, 0:512], OP.mult)
            tt(tmp1[:], tmp1[:], pb[5][:, 0:512], OP.add, sem=sdv)
            B.wt("dve", sdv)
            red(mx1c[:, 8 * t + m:8 * t + m + 1], tmp1[:], OP.max, absval=True, sem=sdv)
            xslot_free[k] = [(sdv, sdv.n)]
        bank_release(4, sdv); bank_release(5, sdv)

    dcc, cc1s = g("dcc"), g("cc1")
    B.wt("dve", sdv)
    red(mxp[:, 0:1], mx1c[:], OP.max, absval=False, sem=sdv)
    n_mx = sdv.n
    bank_acquire(6, "pe")
    B.wt("pe", sdv, n_mx)
    tp(pb[6][0:1, 0:128], mxp[:, 0:1], c_idf[:], sem=spe)
    B.wt("dve", spe)
    cp("dve", rowap(3)[0:1, 0:128], pb[6][0:1, 0:128], sem=sdv)
    bank_release(6, sdv)
    B.wt("dve", sdv)
    red(rowap(3)[0:1, 128:129], rowap(3)[0:1, 0:128], OP.max, absval=False, sem=sdv)
    B.wt("sp", sdv)
    B.dma("sp", dcc, cc1_in, rowap(3)[0:1, 128:129])
    B.wt("pl", dcc)
    cc("AllReduce", OP.max, [cc1_in], [cc1_out], cc1s)
    B.wt("sp", cc1s, 1)
    B.dma("sp", dcc, rowap(3)[0:1, 129:130], cc1_out)
    B.wt("dve", dcc)
    tt(rowap(3)[0:1, 130:131], rowap(3)[0:1, 129:130], scal[0:1, 2:3], OP.mult)
    ts(rowap(3)[0:1, 130:131], rowap(3)[0:1, 130:131], cst(C_R127), None, OP.mult, sem=sdv)
    B.wt("dve", sdv)
    mset("dve", rowap(3)[0:1, 136:144], 1.0, sem=sdv)
    B.wt("dve", sdv)
    cp("dve", rowap(3)[0:1, 136:137], rowap(3)[0:1, 130:131], sem=sdv)
    B.wt("dve", sdv)
    B.op("dve", lambda E: E.reciprocal(rowap(3)[0:1, 144:152], rowap(3)[0:1, 136:144]), sem=sdv)
    n_rx = sdv.n
    bank_acquire(6, "pe")
    B.wt("pe", sdv, n_rx)
    mm(pb[6][:, 0:1], c_ones1[:], rowap(3)[0:1, 130:131], True, True)
    mm(pb[6][:, 1:2], c_ones1[:], rowap(3)[0:1, 144:145], True, True, sem=spe)
    B.wt("dve", spe)
    cp("dve", scal[:, 4:5], pb[6][:, 0:1], sem=sdv)
    cp("dve", scal[:, 5:6], pb[6][:, 1:2], sem=sdv)
    bank_release(6, sdv)
    n_sx = sdv.n

    for snm in ("dxa", "dwi", "dwo", "dab", "dzo", "dzi", "dxo", "dxi", "dst"):
        for e in ENGS:
            B.wt(e, g(snm))
    esB.close()
    esC = ExitStack()
    sbC = lambda name, shape, dt: esC.enter_context(nc.sbuf_tensor(name, shape, dt))
    xq = sbC("xq", [128, CHH * 512], BF16)
    qv = sbC("qv", [128, 512], F32)
    qi = sbC("qi", [128, 512], I32)
    qf = sbC("qf", [128, 512], F32)
    qg = sbC("qg", [128, 512], F32)
    glb = sbC("glb", [128, 2 * CHH * 128], BF16)
    ulb = sbC("ulb", [128, 2 * CHH * 128], BF16)
    sgt = sbC("sgt", [128, 512], F32)
    hbf = sbC("hbf", [128, 2 * 512], F32)
    mxh = sbC("mxh", [128, 24], F32)
    hqt = sbC("hqt", [128, CHF * 512], BF16)
    hin = sbC("hin", [128, 2 * 512], F32)
    dlb = sbC("dlb", [128, 2 * CHF * 128], BF16)
    yst = sbC("yst", [128, 2 * 512], F32)
    xm_pool = sbC("xm_pool", [128, 2 * 512], F32)
    ln1r = sbC("ln1r", [1, 2 * 512], F32)

    dho, dhi, dgl2, dy = g("dho"), g("dhi"), g("dgl"), g("dy")
    mset("dve", mxh[:], 0.0)
    mset("dve", mxp[:, 1:2], 0.0, sem=sdv)
    B.wt("dve", sdv)

    gl_free = [None, None]
    ul_free = [None, None]
    hb_free = [None, None]
    xm_free = [None, None]
    hin_free = [None, None]
    dl_free = [None, None]
    yst_free = [None, None]

    for t in range(NT):
        tc0 = t * T
        B.dma("sp", dst, ln1r[0:1, 0:512], t_stash[0:1, tc0:tc0 + 512])
        B.dma("sp", dst, ln1r[0:1, 512:1024], t_stash[1:2, tc0:tc0 + 512])
        nst = dst.n
        bank_acquire(4, "pe"); bank_acquire(5, "pe")
        B.wt("pe", dst, nst)
        mm(pb[4][:, 0:512], c_ones1[:], ln1r[0:1, 0:512], True, True)
        mm(pb[5][:, 0:512], c_ones1[:], ln1r[0:1, 512:1024], True, True, sem=spe)
        n_bb = spe.n
        for m in range(CHH):
            sl = m % 2
            if xm_free[sl] is not None:
                s, v = xm_free[sl]
                B.wt("sp", s, v)
            B.dma("sp", dxi, xm_pool[:, 512 * sl:512 * sl + 512],
                  t_xmid[128 * m:128 * m + 128, tc0:tc0 + 512])
            B.wt("dve", dxi)
            B.wt("dve", spe, n_bb)
            B.wt("dve", sdv, n_sx)
            tt(qv[:], xm_pool[:, 512 * sl:512 * sl + 512], pb[4][:, 0:512], OP.mult)
            tt(qv[:], qv[:], pb[5][:, 0:512], OP.add, sem=sdv)
            xm_free[sl] = (sdv, sdv.n)
            ts(qv[:], qv[:], scal[:, 5:6], cstp(C_HALF), OP.mult, OP.add)
            cp("dve", qi[:], qv[:])
            cp("dve", qf[:], qi[:])
            tt(qg[:], qf[:], qv[:], OP.is_gt)
            tt(qf[:], qf[:], qg[:], OP.subtract)
            ts(qf[:], qf[:], cstp(C_N128), 127.0, OP.max, OP.min)
            ts(xq[:, 512 * m:512 * m + 512], qf[:], scal[:, 4:5], None, OP.mult, sem=sdv)
        n_xq = dve_fence()
        bank_release(4, sdv); bank_release(5, sdv)
        for m in range(CHF):
            sl = m % 2
            for (buf, free, dram) in ((glb, gl_free, t_glhs), (ulb, ul_free, t_ulhs)):
                if free[sl] is not None:
                    s, v = free[sl]
                    B.wt("pl", s, v)
                B.dma("pl", dgl2, buf[:, 1024 * sl:1024 * sl + 1024],
                      dram.rearrange("k p m -> p k m")[:, :, 128 * m:128 * m + 128])
            ndgl = dgl2.n
            bg = 4 + (m % 2)
            bu = 6 + (m % 2)
            bank_acquire(bg, "pe"); bank_acquire(bu, "pe")
            B.wt("pe", dgl2, ndgl)
            B.wt("pe", sfen, n_xq)
            for kk in range(CHH):
                mm(pb[bg][:, 0:512], glb[:, 1024 * sl + 128 * kk:1024 * sl + 128 * kk + 128],
                   xq[:, 512 * kk:512 * kk + 512], kk == 0, kk == CHH - 1)
            for kk in range(CHH):
                mm(pb[bu][:, 0:512], ulb[:, 1024 * sl + 128 * kk:1024 * sl + 128 * kk + 128],
                   xq[:, 512 * kk:512 * kk + 512], kk == 0, kk == CHH - 1,
                   sem=spe if kk == CHH - 1 else None)
            gl_free[sl] = (spe, spe.n)
            ul_free[sl] = (spe, spe.n)
            B.wt("act", spe)
            B.wt("act", sdv)
            act(sgt[:], pb[bg][:, 0:512], AF.Silu, sem=sac)
            bank_release(bg, sac)
            hsl = m % 2
            if hb_free[hsl] is not None:
                s, v = hb_free[hsl]
                B.wt("dve", s, v)
            B.wt("dve", sac)
            tt(hbf[:, 512 * hsl:512 * hsl + 512], sgt[:], pb[bu][:, 0:512],
               OP.mult, sem=sdv)
            bank_release(bu, sdv)
            B.wt("dve", sdv)
            red(mxh[:, m:m + 1], hbf[:, 512 * hsl:512 * hsl + 512], OP.max,
                absval=True, sem=sdv)
            B.wt("pl", sdv)
            B.dma("pl", dho, t_hd[128 * m:128 * m + 128, tc0:tc0 + 512],
                  hbf[:, 512 * hsl:512 * hsl + 512])
            hb_free[hsl] = (dho, dho.n)
        B.wt("dve", sdv)
        red(mxp[:, 2:3], mxh[:], OP.max, absval=False, sem=sdv)
        B.wt("dve", sdv)
        tt(mxp[:, 1:2], mxp[:, 1:2], mxp[:, 2:3], OP.max, sem=sdv)

    cc2s = g("cc2")
    bank_acquire(6, "pe")
    B.wt("pe", sdv)
    tp(pb[6][0:1, 0:128], mxp[:, 1:2], c_idf[:], sem=spe)
    B.wt("dve", spe)
    cp("dve", ln1r[0:1, 0:128], pb[6][0:1, 0:128], sem=sdv)
    bank_release(6, sdv)
    B.wt("dve", sdv)
    red(ln1r[0:1, 128:129], ln1r[0:1, 0:128], OP.max, absval=False, sem=sdv)
    B.wt("sp", sdv)
    B.dma("sp", dcc, cc2_in, ln1r[0:1, 128:129])
    B.wt("pl", dcc)
    cc("AllReduce", OP.max, [cc2_in], [cc2_out], cc2s)
    B.wt("sp", cc2s, 1)
    B.dma("sp", dcc, ln1r[0:1, 129:130], cc2_out)
    B.wt("dve", dcc)
    tt(ln1r[0:1, 130:131], ln1r[0:1, 129:130], scal[0:1, 3:4], OP.mult)
    ts(ln1r[0:1, 130:131], ln1r[0:1, 130:131], cst(C_R127), None, OP.mult, sem=sdv)
    B.wt("dve", sdv)
    mset("dve", ln1r[0:1, 136:144], 1.0, sem=sdv)
    B.wt("dve", sdv)
    cp("dve", ln1r[0:1, 136:137], ln1r[0:1, 130:131], sem=sdv)
    B.wt("dve", sdv)
    B.op("dve", lambda E: E.reciprocal(ln1r[0:1, 144:152], ln1r[0:1, 136:144]), sem=sdv)
    n_rh = sdv.n
    bank_acquire(6, "pe")
    B.wt("pe", sdv, n_rh)
    mm(pb[6][:, 0:1], c_ones1[:], ln1r[0:1, 130:131], True, True)
    mm(pb[6][:, 1:2], c_ones1[:], ln1r[0:1, 144:145], True, True, sem=spe)
    B.wt("dve", spe)
    cp("dve", scal[:, 6:7], pb[6][:, 0:1], sem=sdv)
    cp("dve", scal[:, 7:8], pb[6][:, 1:2], sem=sdv)
    bank_release(6, sdv)
    n_sh = sdv.n
    if DBG:
        B.wt("pl", sdv, n_sh)
        B.dma("pl", g("ddb"), t_dbg7, scal[:])
        B.wt("pl", g("ddb"))

    for t in range(NT):
        tc0 = t * T
        for kk in range(CHF):
            sl = kk % 2
            if hin_free[sl] is not None:
                s, v = hin_free[sl]
                B.wt("sp", s, v)
            B.wt("sp", dho)
            B.dma("sp", dhi, hin[:, 512 * sl:512 * sl + 512],
                  t_hd[128 * kk:128 * kk + 128, tc0:tc0 + 512])
            B.wt("dve", dhi)
            B.wt("dve", sdv, n_sh)
            ts(qv[:], hin[:, 512 * sl:512 * sl + 512], scal[:, 7:8], cstp(C_HALF),
               OP.mult, OP.add, sem=sdv)
            hin_free[sl] = (sdv, sdv.n)
            cp("dve", qi[:], qv[:])
            cp("dve", qf[:], qi[:])
            tt(qg[:], qf[:], qv[:], OP.is_gt)
            tt(qf[:], qf[:], qg[:], OP.subtract)
            ts(hqt[:, 512 * kk:512 * kk + 512], qf[:], cstp(C_N128), 127.0,
               OP.max, OP.min, sem=sdv)
        n_hq = dve_fence()
        for m in range(CHH):
            sl = m % 2
            if dl_free[sl] is not None:
                s, v = dl_free[sl]
                B.wt("pl", s, v)
            B.dma("pl", dgl2, dlb[:, 2816 * sl:2816 * sl + 2816],
                  t_dlhs.rearrange("k p m -> p k m")[:, :, 128 * m:128 * m + 128])
            ndl = dgl2.n
            bm = 4 + (m % 2)
            bank_acquire(bm, "pe")
            B.wt("pe", dgl2, ndl)
            B.wt("pe", sfen, n_hq)
            for kk in range(CHF):
                mm(pb[bm][:, 0:512],
                   dlb[:, 2816 * sl + 128 * kk:2816 * sl + 128 * kk + 128],
                   hqt[:, 512 * kk:512 * kk + 512], kk == 0, kk == CHF - 1,
                   sem=spe if kk == CHF - 1 else None)
            dl_free[sl] = (spe, spe.n)
            ysl = m % 2
            if yst_free[ysl] is not None:
                s, v = yst_free[ysl]
                B.wt("sp", s, v)
            if xm_free[ysl] is not None:
                s, v = xm_free[ysl]
                B.wt("sp", s, v)
            B.dma("sp", dxi, xm_pool[:, 512 * ysl:512 * ysl + 512],
                  t_xmid[128 * m:128 * m + 128, tc0:tc0 + 512])
            B.wt("dve", dxi)
            B.wt("dve", spe)
            stt(yst[:, 512 * ysl:512 * ysl + 512], pb[bm][:, 0:512],
                scal[:, 6:7], xm_pool[:, 512 * ysl:512 * ysl + 512],
                OP.mult, OP.add, sem=sdv)
            bank_release(bm, sdv)
            xm_free[ysl] = (sdv, sdv.n)
            B.wt("sp", sdv)
            B.dma("sp", dy, d_y[128 * m:128 * m + 128, tc0:tc0 + 512],
                  yst[:, 512 * ysl:512 * ysl + 512])
            yst_free[ysl] = (dy, dy.n)
    B.wt("sp", dy, dy.n)
    B.wt("pl", dy, dy.n)

    with nc.Block() as block:
        @block.sync
        def _(E):
            for f in B.em["sp"]:
                f(E)

        @block.tensor
        def _(E):
            for f in B.em["pe"]:
                f(E)

        @block.vector
        def _(E):
            for f in B.em["dve"]:
                f(E)

        @block.scalar
        def _(E):
            for f in B.em["act"]:
                f(E)

        @block.gpsimd
        def _(E):
            for f in B.em["pl"]:
                f(E)

    esC.close()
    es.close()
    return nc


def kernel(L0_attn_x, L0_attn_W_in, L0_attn_conv_w, L0_attn_ln_gamma,
           L0_attn_ln_beta, L0_attn_W_xproj, L0_attn_A_param, L0_attn_D,
           L0_attn_W_out, L0_ffn_clip_x, L0_ffn_clip_down, L0_ffn_w_gate,
           L0_ffn_w_up, L0_ffn_w_down):
    xf = np.asarray(L0_attn_x, np.float32)
    w_in = np.ascontiguousarray(np.asarray(L0_attn_W_in, np.float32))
    conv_w = np.asarray(L0_attn_conv_w, np.float32).reshape(CI, 4)
    conv_t = np.ascontiguousarray(conv_w.reshape(CHI, 128, 4).transpose(1, 0, 2).reshape(128, CHI * 4))
    gam_t = np.ascontiguousarray(np.asarray(L0_attn_ln_gamma, np.float32).reshape(CHI, 128).T)
    bet_t = np.ascontiguousarray(np.asarray(L0_attn_ln_beta, np.float32).reshape(CHI, 128).T)
    wxp = np.asarray(L0_attn_W_xproj, np.float32)
    wxp_pad = np.zeros((CI, 80), np.float32)
    wxp_pad[:, 0:1] = wxp[:, 0:1]
    wxp_pad[:, 32:48] = wxp[:, 1:17]
    wxp_pad[:, 64:80] = wxp[:, 17:33]
    wxp_t = np.ascontiguousarray(wxp_pad.reshape(CHI, 128, 80).transpose(1, 0, 2).reshape(128, CHI * 80))
    a_t = np.ascontiguousarray(np.asarray(L0_attn_A_param, np.float32).T)
    d_row = np.ascontiguousarray(np.asarray(L0_attn_D, np.float32).reshape(1, CI))
    w_out = np.ascontiguousarray(np.asarray(L0_attn_W_out, np.float32))
    w_gate = np.ascontiguousarray(np.asarray(L0_ffn_w_gate, np.float32))
    w_up = np.ascontiguousarray(np.asarray(L0_ffn_w_up, np.float32))
    w_down = np.ascontiguousarray(np.asarray(L0_ffn_w_down, np.float32))
    cx = np.asarray(L0_ffn_clip_x, np.float32).reshape(1, 1)
    cd = np.asarray(L0_ffn_clip_down, np.float32).reshape(1, 1)
    idf = np.eye(128, dtype=np.float32)
    idb = idf.astype(ml_dtypes.bfloat16)
    ones1 = np.ones((1, 128), np.float32)
    consts = np.broadcast_to(np.asarray(CONSTVALS, np.float32), (128, len(CONSTVALS))).copy()

    nc = build_program()
    in_maps = []
    for b in range(8):
        m = {
            "x": np.ascontiguousarray(xf[b].T),
            "w_in": w_in, "conv_w": conv_t, "gamma_t": gam_t, "beta_t": bet_t,
            "d_row": d_row, "wxp_t": wxp_t, "a_t": a_t, "w_out": w_out,
            "clip_x": cx, "clip_dn": cd, "w_gate": w_gate, "w_up": w_up,
            "w_down": w_down, "idf": idf, "idb": idb, "ones1": ones1,
            "consts": consts,
        }
        in_maps.append(m)
    res = run_bass_kernel_spmd(nc, in_maps, core_ids=list(range(8)))
    out = np.stack([np.asarray(res.results[b]["y"], np.float32).T for b in range(8)])
    return out


# version 17
# speedup vs baseline: 1.0705x; 1.0705x over previous
import os
import numpy as np
import ml_dtypes
import concourse.bass as bass
import concourse.mybir as mybir
from concourse.bass_utils import run_bass_kernel_spmd
from contextlib import ExitStack

F32 = mybir.dt.float32
F32R = mybir.dt.float32r
BF16 = mybir.dt.bfloat16
I32 = mybir.dt.int32
AF = mybir.ActivationFunctionType
OP = mybir.AluOpType
AX = mybir.AxisListType.X

S = 2048
H = 1024
CI = 2048
IF = 2816
NT = 4
T = 512
NTC = 4
NQ = 8
QW = 256
NN = 16
CHH = H // 128
CHI = CI // 128
CHF = IF // 128

C_EPS, C_NH, C_1024, C_2048, C_127, C_R127, C_HALF, C_NEG1, C_ONE, C_N128 = range(10)
CONSTVALS = [1e-5, -0.5, 1.0 / 1024.0, 1.0 / 2048.0, 127.0, 1.0 / 127.0, 0.5, -1.0, 1.0, -128.0]

ENGS = ("sp", "pe", "dve", "act", "pl")


class Sem:
    def __init__(self, h):
        self.h = h
        self.n = 0


class Builder:
    def __init__(self, nc):
        self.nc = nc
        self.em = {e: [] for e in ENGS}

    def on(self, eng, fn):
        self.em[eng].append(fn)

    def wt(self, eng, sem, v=None):
        v = sem.n if v is None else v
        if v <= 0:
            return
        self.on(eng, lambda E, h=sem.h, vv=v: E.wait_ge(h, vv))

    def dma(self, eng, sem, out, in_):
        self.on(eng, lambda E, o=out, i=in_, h=sem.h: E.dma_start(out=o, in_=i).then_inc(h, 16))
        sem.n += 16
        return sem.n

    def op(self, eng, fn, sem=None, drain=True):
        if sem is None:
            self.on(eng, fn)
            return None
        sem.n += 1
        if eng in ("dve", "act"):
            self.on(eng, fn)
            self.on(eng, lambda E, h=sem.h: E.drain().then_inc(h, 1))
        else:
            self.on(eng, lambda E, f=fn, h=sem.h: f(E).then_inc(h, 1))
        return sem.n


def build_program():
    nc = bass.Bass("TRN2", target_bir_lowering=False, detect_race_conditions=False)
    B = Builder(nc)

    di = {}
    def d_in(name, shape, dt=F32):
        di[name] = nc.dram_tensor(name, shape, dt, kind="ExternalInput").ap()
        return di[name]

    d_x = d_in("x", [H, S])
    d_win = d_in("w_in", [H, 4096])
    d_cw = d_in("conv_w", [128, CHI * 4])
    d_gam = d_in("gamma_t", [128, CHI])
    d_bet = d_in("beta_t", [128, CHI])
    d_drow = d_in("d_row", [1, CI])
    d_wxp = d_in("wxp_t", [128, CHI * 80])
    d_at = d_in("a_t", [16, CI])
    d_wout = d_in("w_out", [CI, H])
    d_cx = d_in("clip_x", [1, 1])
    d_cd = d_in("clip_dn", [1, 1])
    d_wg = d_in("w_gate", [IF, H])
    d_wu = d_in("w_up", [IF, H])
    d_wd = d_in("w_down", [H, IF])
    d_idf = d_in("idf", [128, 128])
    d_idb = d_in("idb", [128, 128], BF16)
    d_ones1 = d_in("ones1", [1, 128])
    d_cst = d_in("consts", [128, len(CONSTVALS)])
    d_y = nc.dram_tensor("y", [H, S], F32, kind="ExternalOutput").ap()

    DBG = bool(os.environ.get("KDBG"))
    dbk = dict(kind="ExternalOutput") if DBG else {}
    t_winb = nc.dram_tensor("winb_d", [H, 4096], BF16).ap()
    t_woutb = nc.dram_tensor("woutb_d", [CI, H], BF16).ap()
    t_abig = nc.dram_tensor("abig_d", [128 * NN * CI], F32).ap()
    t_zsp = nc.dram_tensor("zsp_d", [CI, S], F32).ap()
    t_xmid = nc.dram_tensor("xmid_d", [H, S], F32).ap()
    t_hd = nc.dram_tensor("h_d", [IF, S], F32, **(dict(kind="ExternalOutput") if bool(os.environ.get("KDBG")) else {})).ap()
    t_stash = nc.dram_tensor("stash_d", [2, S], F32).ap()
    t_glhs = nc.dram_tensor("glhs_d", [CHH, 128, IF], BF16).ap()
    t_ulhs = nc.dram_tensor("ulhs_d", [CHH, 128, IF], BF16).ap()
    t_dlhs = nc.dram_tensor("dlhs_d", [CHF, 128, H], BF16).ap()
    t_dbg1 = nc.dram_tensor("dbg1", [CI, S], F32, **dbk).ap()
    t_dbg2 = nc.dram_tensor("dbg2", [CI, S], F32, **dbk).ap()
    t_dbg3 = nc.dram_tensor("dbg3", [8, S], F32, **dbk).ap()
    t_dbg4 = nc.dram_tensor("dbg4", [16, S], F32, **dbk).ap()
    t_dbg5 = nc.dram_tensor("dbg5", [CI, S], F32, **dbk).ap()
    t_dbg6 = nc.dram_tensor("dbg6", [H, S], F32, **dbk).ap()
    t_dbg7 = nc.dram_tensor("dbg7", [128, 16], F32, **dbk).ap()
    cc1_in = nc.dram_tensor("cc1_in", [1, 1], F32).ap()
    cc1_out = nc.dram_tensor("cc1_out", [1, 1], F32).ap()
    cc2_in = nc.dram_tensor("cc2_in", [1, 1], F32).ap()
    cc2_out = nc.dram_tensor("cc2_out", [1, 1], F32).ap()

    es = ExitStack()
    sb = lambda name, shape, dt: es.enter_context(nc.sbuf_tensor(name, shape, dt))

    c_idf = sb("c_idf", [128, 128], F32)
    c_idb = sb("c_idb", [128, 128], BF16)
    c_ones1 = sb("c_ones1", [1, 128], F32)
    c_ones1r = sb("c_ones1r", [1, 128], F32R)
    c_onescolr = sb("c_onescolr", [128, 1], F32R)
    c_cst = sb("c_cst", [128, len(CONSTVALS)], F32)
    cw_sb = sb("cw_sb", [128, CHI * 4], F32)
    gam_sb = sb("gam_sb", [128, CHI], F32)
    bet_sb = sb("bet_sb", [128, CHI], F32)
    wxp_sb = sb("wxp_sb", [128, CHI * 80], F32)
    wxpW = sb("wxpW", [128, CHI * 80], F32R)
    dbc_sb = sb("dbc_sb", [128, CI], BF16)
    rows = sb("rows", [1, 6 * 512], F32)
    rows_r = sb("rows_r", [1, 1280], F32R)
    scal = sb("scal", [128, 16], F32)
    mx1c = sb("mx1c", [128, 32], F32)
    mxp = sb("mxp", [128, 8], F32)
    pw = sb("pw", [128, 1408], F32)
    pt = sb("pt", [128, 512], F32)
    pi = sb("pi", [128, 512], I32)
    pf = sb("pf", [128, 512], F32)
    pg = sb("pg", [128, 512], F32)
    pq = sb("pq", [128, 1408], BF16)
    pev = sb("pev", [128, 2 * 128], BF16)
    prow = sb("prow", [128, 24], F32)

    pb = [es.enter_context(nc.psum_tensor(f"pb{i}", [128, 512], F32)) for i in range(8)]

    semnames = ["dxa", "dwi", "dwo", "dab", "dzo", "dzi", "dxo", "dxi", "dst",
                "dho", "dhi", "dgl", "dpw", "dy", "dmc", "dwb", "dcc", "ddb",
                "spe", "sdv", "sac", "spl", "sfen", "cc1", "cc2", "cc3"]
    SM = {n: Sem(es.enter_context(nc.semaphore(n))) for n in semnames}
    g = lambda n: SM[n]
    spe, sdv, sac, spl = g("spe"), g("sdv"), g("sac"), g("spl")

    esB = ExitStack()
    sbB = lambda name, shape, dt: esB.enter_context(nc.sbuf_tensor(name, shape, dt))
    x_pool = sbB("x_pool", [128, 3 * 512], F32)
    ln0bf = sbB("ln0bf", [128, CHH * 512], BF16)
    xr_sl = sbB("xr_sl", [128, 512], F32R)
    sq_sl = sbB("sq_sl", [128, 512], F32R)
    winb = sbB("winb", [128, 2 * CHH * 128], BF16)
    xcraw = sbB("xcraw", [128, 3 * 516], BF16)
    tails = sbB("tails", [128, CHI * 4], F32)
    cvd = sbB("cvd", [128, 8 * 128], BF16)
    xcs = sbB("xcs", [128, CHI * 512], F32)
    xcsr = sbB("xcsr", [128, 2 * 512], F32R)
    szsl = sbB("szsl", [128, 2 * 512], F32)
    xpsb = sbB("xpsb", [128, 512], F32)
    usb = None
    dcol = sbB("dcol", [128, 8], F32)
    uT = sbB("uT", [128, 64], F32)
    diag = sbB("diag", [128, 64 * 128], BF16)
    abc = sbB("abc", [128, 2 * 4096], F32)
    ebf = sbB("ebf", [128, 4096], BF16)
    stm = sbB("stm", [128, 256], F32)
    sfm = sbB("sfm", [128, 2 * 512], F32)
    gbf = sbB("gbf", [128, CHI * 512], BF16)
    woutb = sbB("woutb", [128, 2 * CHI * 128], BF16)
    tmp1 = sbB("tmp1", [128, 512], F32)
    usb = tmp1

    sfen = g("sfen")

    def dve_fence():
        sfen.n += 1
        B.on("dve", lambda E, h=sfen.h: E.drain().then_inc(h, 1))
        return sfen.n

    def mm(out, lhsT, rhs, start, stop, sem=None):
        return B.op("pe", lambda E, o=out, l=lhsT, r=rhs, a=start, b=stop:
                    E.matmul(o, l, r, start=a, stop=b), sem)

    def tp(out, in_, ident, sem=None):
        return B.op("pe", lambda E, o=out, i=in_, d=ident: E.transpose(o, i, d), sem)

    def cp(eng, out, in_, sem=None, drain=True):
        return B.op(eng, lambda E, o=out, i=in_: E.tensor_copy(o, i), sem, drain)

    def tt(out, a, b, o, sem=None, eng="dve", drain=True):
        return B.op(eng, lambda E, x=out, p=a, q=b, r=o: E.tensor_tensor(x, p, q, r), sem, drain)

    def ts(out, in_, s1, s2, o1, o2=None, sem=None, eng="dve", drain=True):
        if s2 is None:
            return B.op(eng, lambda E, x=out, i=in_, a=s1, p=o1:
                        E.tensor_scalar(x, i, a, None, p), sem, drain)
        return B.op(eng, lambda E, x=out, i=in_, a=s1, b=s2, p=o1, q=o2:
                    E.tensor_scalar(x, i, a, b, p, q), sem, drain)

    def stt(out, in0, s, in1, o1, o2, sem=None, eng="dve", drain=True):
        return B.op(eng, lambda E, x=out, i=in0, a=s, j=in1, p=o1, q=o2:
                    E.scalar_tensor_tensor(x, i, a, j, p, q), sem, drain)

    def act(out, in_, fn, bias=0.0, scale=1.0, accum=None, sem=None):
        return B.op("act", lambda E, o=out, i=in_, f=fn, b=bias, s=scale, a=accum:
                    E.activation(o, i, f, bias=b, scale=s, accum_out=a), sem)

    def red(out, in_, o, absval=False, sem=None, drain=True):
        return B.op("dve", lambda E, x=out, i=in_, p=o, a=absval:
                    E.tensor_reduce(x, i, AX, p, apply_absolute_value=a), sem, drain)

    def mset(eng, ap_, v, sem=None, drain=True):
        return B.op(eng, lambda E, a=ap_, vv=v: E.memset(a, vv), sem, drain)

    def cc(kind, op_, ins, outs, sem):
        sem.n += 1
        B.on("pl", lambda E, k=kind, o=op_, i=ins, u=outs, h=sem.h:
             E.collective_compute(k, o, replica_groups=[list(range(8))],
                                  ins=i, outs=u).then_inc(h, 1))
        return sem.n

    bank_free = [None] * 8

    def bank_acquire(i, eng):
        if bank_free[i] is not None:
            s, v = bank_free[i]
            B.wt(eng, s, v)
            bank_free[i] = None

    def bank_release(i, sem):
        bank_free[i] = (sem, sem.n)

    rowap = lambda i: rows[0:1, 512 * i:512 * i + 512]
    row1 = lambda i, w=1: rows[0:1, 512 * i:512 * i + w]
    cst = lambda c: c_cst[0:1, c:c + 1]
    cstp = lambda c: c_cst[:, c:c + 1]

    dmc, dwb = g("dmc"), g("dwb")
    for out, src in [(c_idf[:], d_idf), (c_idb[:], d_idb), (c_ones1[:], d_ones1),
                     (c_cst[:], d_cst), (cw_sb[:], d_cw), (gam_sb[:], d_gam),
                     (bet_sb[:], d_bet), (wxp_sb[:], d_wxp)]:
        B.dma("sp", dmc, out, src)
    B.dma("sp", dmc, scal[0:1, 0:1], d_cx)
    B.dma("sp", dmc, scal[0:1, 1:2], d_cd)
    B.dma("sp", dmc, xcs[0:16, 0:CI], d_at)

    n_dmc0 = dmc.n
    B.wt("pe", dmc, n_dmc0)
    B.wt("act", dmc, n_dmc0)
    act(scal[0:1, 2:3], scal[0:1, 0:1], AF.Sigmoid)
    act(scal[0:1, 3:4], scal[0:1, 1:2], AF.Sigmoid, sem=sac)

    B.wt("dve", dmc, n_dmc0)
    cp("dve", c_ones1r[:], c_ones1[:])
    cp("dve", c_onescolr[:], c_cst[:, C_ONE:C_ONE + 1])
    mset("dve", rowap(0), 1.0, sem=sdv)
    B.wt("dve", sdv)
    cp("dve", rows_r[0:1, 0:512], rowap(0), sem=sdv)
    n_ones_r = sdv.n

    at = xcs[0:16, 0:CI]
    a_abs = xcs[32:48, 0:CI]
    a_rel = xcs[32:48, CI:2 * CI]
    B.wt("act", sdv, n_ones_r)
    act(a_abs, at, AF.Abs, sem=sac)
    B.wt("act", sac)
    act(a_abs, a_abs, AF.Exp, scale=c_cst[0:16, C_NEG1:C_NEG1 + 1], sem=sac)
    B.wt("act", sac)
    act(a_abs, a_abs, AF.Ln, bias=c_cst[0:16, C_ONE:C_ONE + 1], sem=sac)
    B.wt("act", sac)
    act(a_rel, at, AF.Relu, sem=sac)
    n_aact = sac.n
    B.wt("dve", sac, n_aact)
    tt(a_abs, a_abs, a_rel, OP.add)
    ts(a_abs, a_abs, 0.1, 10.0, OP.max, OP.min)
    ts(a_abs, a_abs, -1.0, None, OP.mult, sem=sdv)
    n_aprime = sdv.n
    B.wt("sp", sdv, n_aprime)
    B.dma("sp", dwb, t_abig[0:NN * CI].rearrange("(a b) -> a b", a=16), a_abs)
    nsz = NN * CI
    for _ in range(7):
        B.wt("sp", dwb)
        B.dma("sp", dwb, t_abig[nsz:2 * nsz], t_abig[0:nsz])
        nsz *= 2
    B.wt("sp", dwb)
    n_abig = dwb.n

    B.dma("sp", dmc, xcs[96:97, 0:CI], d_drow)
    n_drow = dmc.n
    B.wt("dve", dmc, n_drow)
    cp("dve", ln0bf[0:1, 0:CI], xcs[96:97, 0:CI], sem=sdv)
    n_drowb = sdv.n
    B.wt("dve", dmc, n_dmc0)
    cp("dve", gbf[0:1, 0:128], c_ones1[:], sem=sdv)
    n_ones1b = sdv.n
    B.wt("pe", sdv, max(n_drowb, n_ones1b))
    for kb in range(4):
        bank_acquire(kb, "pe")
    mm(pb[0][:, 0:512], gbf[0:1, 0:128], ln0bf[0:1, 0:512], True, True)
    mm(pb[1][:, 0:512], gbf[0:1, 0:128], ln0bf[0:1, 512:1024], True, True)
    mm(pb[2][:, 0:512], gbf[0:1, 0:128], ln0bf[0:1, 1024:1536], True, True)
    mm(pb[3][:, 0:512], gbf[0:1, 0:128], ln0bf[0:1, 1536:2048], True, True, sem=spe)
    n_dbc = spe.n
    B.wt("dve", spe, n_dbc)
    for k in range(4):
        cp("dve", dbc_sb[:, 512 * k:512 * k + 512], pb[k][:, 0:512])
    B.op("dve", lambda E: E.engine_nop(), sem=sdv)
    n_dbcs = sdv.n
    for k in range(4):
        bank_release(k, sdv)

    B.wt("dve", dmc, n_dmc0)
    for c in range(CHI):
        ts(wxpW[:, 80 * c:80 * c + 80], wxp_sb[:, 80 * c:80 * c + 80],
           gam_sb[:, c:c + 1], None, OP.mult, sem=sdv if c == CHI - 1 else None)
    n_wxpw = sdv.n
    B.wt("pe", sdv, n_wxpw)
    bank_acquire(4, "pe"); bank_acquire(5, "pe")
    for c in range(CHI):
        mm(pb[4][0:80, 0:1], wxp_sb[:, 80 * c:80 * c + 80], gam_sb[:, c:c + 1], c == 0, c == CHI - 1)
    for c in range(CHI):
        mm(pb[5][0:80, 0:1], wxp_sb[:, 80 * c:80 * c + 80], bet_sb[:, c:c + 1],
           c == 0, c == CHI - 1, sem=spe if c == CHI - 1 else None)
    n_cgb = spe.n
    B.wt("dve", spe, n_cgb)
    cp("dve", xpsb[0:80, 0:1], pb[4][0:80, 0:1])
    cp("dve", xpsb[0:80, 1:2], pb[5][0:80, 0:1], sem=sdv)
    n_cgev = sdv.n
    bank_release(4, sdv); bank_release(5, sdv)
    B.wt("pe", sdv, n_cgev)
    bank_acquire(7, "pe")
    tp(pb[7][0:1, 0:80], xpsb[0:80, 0:1], c_idf[0:80, 0:80], sem=spe)
    n_cgt = spe.n
    B.wt("dve", spe, n_cgt)
    cp("dve", rows_r[0:1, 512:592], pb[7][0:1, 0:80])
    cp("dve", scal[0:80, 8:9], xpsb[0:80, 1:2], sem=sdv)
    n_cgr = sdv.n
    bank_release(7, sdv)
    cg_r = rows_r[0:1, 512:592]
    cb_col = scal[0:80, 8:9]
    negmur_r = rows_r[0:1, 672:1184]
    ones512_r = rows_r[0:1, 0:512]

    B.wt("sp", sac)
    B.wt("sp", spe)
    B.wt("dve", spe)
    for k in range(CHH):
        B.wt("sp", sdv)
        B.wt("sp", dwb)
        B.dma("sp", dmc, xcs[:, 0:4096], d_win[128 * k:128 * k + 128, :])
        B.wt("dve", dmc)
        B.wt("dve", dwb)
        cp("dve", ln0bf[:, 0:4096], xcs[:, 0:4096], sem=sdv)
        B.wt("sp", sdv)
        B.dma("sp", dwb, t_winb[128 * k:128 * k + 128, :], ln0bf[:, 0:4096])
    for k in range(CHI):
        B.wt("sp", sdv)
        B.wt("sp", dwb)
        B.dma("sp", dmc, xcs[:, 4096:4096 + 1024], d_wout[128 * k:128 * k + 128, :])
        B.wt("dve", dmc)
        B.wt("dve", dwb)
        cp("dve", ln0bf[:, 0:1024], xcs[:, 4096:4096 + 1024], sem=sdv)
        B.wt("sp", sdv)
        B.dma("sp", dwb, t_woutb[128 * k:128 * k + 128, :], ln0bf[:, 0:1024])
    n_wcast = dwb.n

    mset("dve", tails[:], 0.0)
    mset("dve", mx1c[:], 0.0)
    mset("dve", mxp[:], 1.0, sem=sdv)
    n_zeros = sdv.n

    dpw, dgl = g("dpw"), g("dgl")

    def quant_rows(src_ap, ncols):
        mset("dve", prow[:, 0:8], 1.0, sem=sdv, drain=False)
        B.wt("dve", sdv)
        red(prow[:, 0:1], src_ap, OP.max, absval=True, sem=sdv, drain=False)
        B.wt("dve", sdv)
        ts(prow[:, 0:1], prow[:, 0:1], 1e-30, None, OP.max)
        B.op("dve", lambda E: E.reciprocal(prow[:, 8:16], prow[:, 0:8]), sem=sdv, drain=False)
        B.wt("dve", sdv)
        ts(prow[:, 16:17], prow[:, 8:9], cstp(C_127), None, OP.mult)
        ts(prow[:, 17:18], prow[:, 0:1], cstp(C_R127), None, OP.mult)
        for off in range(0, ncols, 512):
            w5 = min(512, ncols - off)
            ts(pt[:, 0:w5], pw[:, off:off + w5], prow[:, 16:17], cstp(C_HALF), OP.mult, OP.add)
            cp("dve", pi[:, 0:w5], pt[:, 0:w5])
            cp("dve", pf[:, 0:w5], pi[:, 0:w5])
            tt(pg[:, 0:w5], pf[:, 0:w5], pt[:, 0:w5], OP.is_gt)
            tt(pf[:, 0:w5], pf[:, 0:w5], pg[:, 0:w5], OP.subtract)
            ts(pq[:, off:off + w5], pf[:, 0:w5], prow[:, 17:18], None, OP.mult,
               sem=sdv, drain=False)
        dve_fence()

    def _prep_half(w_ap, c, off, w5, out_off_fn, j0):
        B.wt("pl", sdv)
        B.dma("pl", dpw, pw[:, 0:w5], w_ap[128 * c:128 * c + 128, off:off + w5])
        B.wt("dve", dpw)
        B.wt("dve", spe)
        for o2 in range(0, w5, 512):
            w6 = min(512, w5 - o2)
            ts(pt[:, 0:w6], pw[:, o2:o2 + w6], prow[:, 16:17], cstp(C_HALF), OP.mult, OP.add)
            cp("dve", pi[:, 0:w6], pt[:, 0:w6])
            cp("dve", pf[:, 0:w6], pi[:, 0:w6])
            tt(pg[:, 0:w6], pf[:, 0:w6], pt[:, 0:w6], OP.is_gt)
            tt(pf[:, 0:w6], pf[:, 0:w6], pg[:, 0:w6], OP.subtract)
            ts(pq[:, o2:o2 + w6], pf[:, 0:w6], prow[:, 17:18], None, OP.mult, sem=sdv)
        dve_fence()
        for j in range(w5 // 128):
            bidx = 6 + (j % 2)
            bank_acquire(bidx, "pe")
            B.wt("pe", sdv)
            B.wt("pe", sfen)
            tp(pb[bidx].bitcast(BF16)[:, 0:128], pq[:, 128 * j:128 * j + 128],
               c_idb[:], sem=spe)
            B.wt("dve", spe)
            B.wt("dve", dgl)
            cp("dve", pev[:, 128 * (j % 2):128 * (j % 2) + 128],
               pb[bidx].bitcast(BF16)[:, 0:128], sem=sdv)
            bank_release(bidx, sdv)
            B.wt("pl", sdv)
            B.dma("pl", dgl, out_off_fn(c, j0 + j),
                  pev[:, 128 * (j % 2):128 * (j % 2) + 128])

    def emit_prep_unit(w_ap, c, ncols, out_off_fn):
        mset("dve", prow[:, 0:8], 1.0, sem=sdv)
        B.wt("dve", sdv)
        mset("dve", prow[:, 18:19], 0.0, sem=sdv)
        for off in range(0, ncols, 1408):
            w5 = min(1408, ncols - off)
            B.wt("pl", sdv)
            B.dma("pl", dpw, pw[:, 0:w5], w_ap[128 * c:128 * c + 128, off:off + w5])
            B.wt("dve", dpw)
            B.wt("dve", sdv)
            red(prow[:, 1:2], pw[:, 0:w5], OP.max, absval=True, sem=sdv)
            B.wt("dve", sdv)
            tt(prow[:, 18:19], prow[:, 18:19], prow[:, 1:2], OP.max, sem=sdv)
            B.wt("dve", sdv)
        ts(prow[:, 0:1], prow[:, 18:19], 1e-30, None, OP.max, sem=sdv)
        B.wt("dve", sdv)
        B.op("dve", lambda E: E.reciprocal(prow[:, 8:16], prow[:, 0:8]), sem=sdv)
        B.wt("dve", sdv)
        ts(prow[:, 16:17], prow[:, 8:9], cstp(C_127), None, OP.mult)
        ts(prow[:, 17:18], prow[:, 0:1], cstp(C_R127), None, OP.mult, sem=sdv)
        for off in range(0, ncols, 1408):
            w5 = min(1408, ncols - off)
            _prep_half(w_ap, c, off, w5, out_off_fn, off // 128)

    prep_units = (
        [(d_wg, c, H, lambda c, j: t_glhs[j, :, 128 * c:128 * c + 128]) for c in range(CHF)]
        + [(d_wu, c, H, lambda c, j: t_ulhs[j, :, 128 * c:128 * c + 128]) for c in range(CHF)]
        + [(d_wd, c, IF, lambda c, j: t_dlhs[j, :, 128 * c:128 * c + 128]) for c in range(CHH)]
    )
    prep_pos = [0]

    def emit_prep(k):
        while k > 0 and prep_pos[0] < len(prep_units):
            w_ap, c, ncols, fn = prep_units[prep_pos[0]]
            emit_prep_unit(w_ap, c, ncols, fn)
            prep_pos[0] += 1
            k -= 1

    dxa, dwi, dwo, dab = g("dxa"), g("dwi"), g("dwo"), g("dab")
    dzo, dzi, dxo, dxi, dst = g("dzo"), g("dzi"), g("dxo"), g("dxi"), g("dst")

    xslot_free = [[] for _ in range(3)]
    xload_k = [0]

    def load_x(rows_lo, cols_lo, src=None, dsem=None):
        k = xload_k[0] % 3
        for s, v in xslot_free[k]:
            B.wt("sp", s, v)
        xslot_free[k] = []
        src = d_x if src is None else src
        dsem = dxa if dsem is None else dsem
        B.dma("sp", dsem, x_pool[:, 512 * k:512 * k + 512],
              src[rows_lo:rows_lo + 128, cols_lo:cols_lo + 512])
        xload_k[0] += 1
        return k, dsem.n

    win_free = [None, None]
    wout_free = [None, None]
    abc_free = [None, None]
    ebf_busy = [None, None]
    xcraw_free = [None] * 3
    cvd_free = [None, None]
    conv_fence = {}
    xcsr_free = [None, None]
    szsl_free = [None, None]
    zsp_written = {}

    for e in ENGS:
        B.wt(e, dwb, dwb.n)
        B.wt(e, g("dmc"), g("dmc").n)
    B.wt("dve", sdv, n_zeros)

    def ln_stats_chunk(src_f32, slot2, bsum, bssq, first, last):
        xr = xr_sl[:, 0:512]
        sq = sq_sl[:, 0:512]
        cp("dve", xr, src_f32, sem=sdv, drain=False)
        tt(sq, xr, xr, OP.mult)
        dve_fence()
        B.wt("pe", sfen)
        mm(pb[bsum][0:1, 0:512], c_onescolr[:], xr, first, last)
        mm(pb[bssq][0:1, 0:512], c_onescolr[:], sq, first, last, sem=spe)

    def ln_rows(bsum, bssq, mu_i, rstd_i, invn_c, tmp_i):
        B.wt("dve", spe)
        B.wt("dve", g("dst"))
        ts(rowap(mu_i), pb[bsum][0:1, 0:512], cst(invn_c), None, OP.mult, sem=sdv)
        ts(rowap(tmp_i), pb[bssq][0:1, 0:512], cst(invn_c), None, OP.mult, sem=sdv)
        bank_release(bsum, sdv); bank_release(bssq, sdv)
        tt(rowap(rstd_i), rowap(mu_i), rowap(mu_i), OP.mult)
        tt(rowap(rstd_i), rowap(tmp_i), rowap(rstd_i), OP.subtract, sem=sdv)
        nf = dve_fence()
        B.wt("act", sfen, nf)
        B.wt("act", sdv)
        act(rowap(rstd_i), rowap(rstd_i), AF.Ln, bias=cst(C_EPS), sem=sac)
        B.wt("act", sac)
        act(rowap(rstd_i), rowap(rstd_i), AF.Exp, scale=cst(C_NH), sem=sac)

    def bcast_rows(row_ap, bidx, sem=None, val=None):
        sem = sac if sem is None else sem
        bank_acquire(bidx, "pe")
        B.wt("pe", sem, sem.n if val is None else val)
        B.wt("pe", sfen)
        mm(pb[bidx][:, 0:512], c_ones1[:], row_ap, True, True, sem=spe)

    for t in range(NT):
        tc0 = t * T
        emit_prep(13)
        bank_acquire(0, "pe"); bank_acquire(1, "pe")
        for c in range(CHH):
            k, dn = load_x(128 * c, tc0)
            B.wt("dve", dxa, dn)
            if xcsr_free[c % 2] is not None:
                s, v = xcsr_free[c % 2]
            B.wt("dve", spe)
            ln_stats_chunk(x_pool[:, 512 * k:512 * k + 512], c % 2, 0, 1,
                           c == 0, c == CHH - 1)
            xslot_free[k] = [(sdv, sdv.n)]
        ln_rows(0, 1, mu_i=0, rstd_i=2, invn_c=C_1024, tmp_i=1)
        bcast_rows(rowap(0), 2)
        bcast_rows(rowap(2), 3)
        n_bc0 = spe.n
        for c in range(CHH):
            k, dn = load_x(128 * c, tc0)
            B.wt("dve", dxa, dn)
            B.wt("dve", spe, n_bc0)
            tt(tmp1[:], x_pool[:, 512 * k:512 * k + 512], pb[2][:, 0:512], OP.subtract)
            tt(ln0bf[:, 512 * c:512 * c + 512], tmp1[:], pb[3][:, 0:512], OP.mult,
               sem=sdv, drain=False)
            xslot_free[k] = [(sdv, sdv.n)]
        n_ln0 = dve_fence()
        bank_release(2, sdv); bank_release(3, sdv)

        bank_acquire(0, "pe"); bank_acquire(1, "pe")
        B.wt("sp", dwb, n_wcast)
        def emit_conv_chunk(c):
            rsl = c % 3
            xcr = xcraw[:, 516 * rsl:516 * rsl + 516]
            dsl = 512 * (c % 2)
            bcv = 2 + (c % 2)
            bank_acquire(bcv, "pe")
            B.wt("pe", sfen, conv_fence[c])
            for kk in range(4):
                mm(pb[bcv][:, 0:512], cvd[:, dsl + 128 * kk:dsl + 128 * kk + 128],
                   xcr[:, kk:kk + 512], kk == 0, kk == 3,
                   sem=spe if kk == 3 else None)
            xcraw_free[rsl] = (spe, spe.n)
            cvd_free[c % 2] = (spe, spe.n)
            B.wt("act", spe)
            act(xcs[:, 512 * c:512 * c + 512], pb[bcv][:, 0:512], AF.Silu, sem=sac)
            bank_release(bcv, sac)
            B.wt("dve", sac)
            if xcsr_free[c % 2] is not None:
                s, v = xcsr_free[c % 2]
                B.wt("dve", s, v)
            B.wt("dve", spe)
            xrr = xcsr[:, 512 * (c % 2):512 * (c % 2) + 512]
            cp("dve", xrr, xcs[:, 512 * c:512 * c + 512], sem=sdv, drain=False)
            sq = sq_sl[:, 0:512]
            tt(sq, xrr, xrr, OP.mult)
            dve_fence()
            B.wt("pe", sfen)
            mm(pb[0][0:1, 0:512], c_onescolr[:], xrr, c == 0, c == CHI - 1)
            mm(pb[1][0:1, 0:512], c_onescolr[:], sq, c == 0, c == CHI - 1)
            if c == 0:
                bank_acquire(7, "pe")
            mm(pb[7][0:80, 0:512], wxpW[:, 80 * c:80 * c + 80], xrr,
               c == 0, False, sem=spe)
            xcsr_free[c % 2] = (spe, spe.n)

        for m in range(32):
            sl = m % 2
            if win_free[sl] is not None:
                s, v = win_free[sl]
                B.wt("sp", s, v)
            B.dma("sp", dwi, winb[:, 1024 * sl:1024 * sl + 1024],
                  t_winb.rearrange("(k p) m -> p k m", p=128)[:, :, 128 * m:128 * m + 128])
            bm = 4 + (m % 2)
            bank_acquire(bm, "pe")
            B.wt("pe", dwi)
            B.wt("pe", sfen, n_ln0)
            for kk in range(CHH):
                mm(pb[bm][:, 0:512], winb[:, 1024 * sl + 128 * kk:1024 * sl + 128 * kk + 128],
                   ln0bf[:, 512 * kk:512 * kk + 512], kk == 0, kk == CHH - 1,
                   sem=spe if kk == CHH - 1 else None)
            win_free[sl] = (spe, spe.n)
            if m < CHI:
                c = m
                rsl = c % 3
                if xcraw_free[rsl] is not None:
                    s, v = xcraw_free[rsl]
                    B.wt("dve", s, v)
                if cvd_free[c % 2] is not None:
                    s, v = cvd_free[c % 2]
                    B.wt("dve", s, v)
                B.wt("dve", spe)
                xcr = xcraw[:, 516 * rsl:516 * rsl + 516]
                dsl = 512 * (c % 2)
                cp("dve", xcr[:, 0:3], tails[:, 4 * c:4 * c + 3])
                cp("dve", xcr[:, 3:515], pb[bm][:, 0:512], sem=sdv, drain=False)
                B.wt("dve", sdv)
                cp("dve", tails[:, 4 * c:4 * c + 3], xcr[:, 512:515])
                for kk in range(4):
                    ts(cvd[:, dsl + 128 * kk:dsl + 128 * kk + 128], c_idb[:],
                       cw_sb[:, 4 * c + kk:4 * c + kk + 1], None, OP.mult)
                conv_fence[c] = dve_fence()
                bank_release(bm, sdv)
                if c >= 1:
                    emit_conv_chunk(c - 1)
            else:
                if m == CHI:
                    emit_conv_chunk(CHI - 1)
                czz = m - CHI
                zsl = czz % 2
                if szsl_free[zsl] is not None:
                    s, v = szsl_free[zsl]
                    B.wt("act", s, v)
                B.wt("act", spe)
                act(szsl[:, 512 * zsl:512 * zsl + 512], pb[bm][:, 0:512], AF.Silu, sem=sac)
                bank_release(bm, sac)
                B.wt("pl", sac)
                B.dma("pl", dzo, t_zsp[128 * czz:128 * czz + 128, tc0:tc0 + 512],
                      szsl[:, 512 * zsl:512 * zsl + 512])
                szsl_free[zsl] = (dzo, dzo.n)
                zsp_written[(t, czz)] = dzo.n
        ln_rows(0, 1, mu_i=0, rstd_i=2, invn_c=C_2048, tmp_i=1)
        B.wt("dve", sac)
        ts(rowap(4), rowap(0), -1.0, None, OP.mult)
        cp("dve", negmur_r, rowap(4), sem=sdv)
        n_nmr = dve_fence()
        B.wt("pe", sfen, n_nmr)
        mm(pb[7][0:80, 0:512], cg_r, negmur_r, False, True, sem=spe)
        n_xp = spe.n
        bcast_rows(rowap(0), 2)
        bcast_rows(rowap(2), 3)
        n_bc2 = spe.n
        B.wt("dve", spe, n_xp)
        cp("dve", xpsb[0:80, 0:512], pb[7][0:80, 0:512])
        B.wt("dve", spe, n_bc2)
        tt(xpsb[0:80, 0:512], xpsb[0:80, 0:512], pb[3][0:80, 0:512], OP.mult)
        ts(xpsb[0:80, 0:512], xpsb[0:80, 0:512], cb_col, None, OP.add, sem=sdv)
        bank_release(7, sdv)
        cp("dve", sfm[0:16, 0:512], xpsb[32:48, 0:512])
        cp("dve", sfm[0:16, 512:1024], xpsb[64:80, 0:512])
        tt(usb[0:16, 0:512], sfm[0:16, 0:512], sfm[0:16, 512:1024], OP.mult, sem=sdv)
        n_u = dve_fence()
        B.wt("act", sfen, n_u)
        act(rowap(5), xpsb[0:1, 0:512], AF.Abs, sem=sac)
        B.wt("act", sac)
        act(rowap(5), rowap(5), AF.Exp, scale=cst(C_NEG1), sem=sac)
        B.wt("act", sac)
        act(rowap(5), rowap(5), AF.Ln, bias=cst(C_ONE), sem=sac)
        B.wt("act", sac)
        act(rowap(1), xpsb[0:1, 0:512], AF.Relu, sem=sac)
        B.wt("dve", sac)
        tt(rowap(5), rowap(5), rowap(1), OP.add)
        ts(rowap(5), rowap(5), 1e-6, 10.0, OP.max, OP.min, sem=sdv)
        n_delta = dve_fence()
        bank_acquire(4, "pe")
        B.wt("pe", sfen)
        for tcg in range(NTC):
            tp(pb[4][:, tcg:tcg + 1], rowap(5)[0:1, 128 * tcg:128 * tcg + 128],
               c_idf[0:1, 0:1])
        for tcg in range(NTC):
            tp(pb[4][:, 16 + 16 * tcg:16 + 16 * tcg + 16],
               usb[0:16, 128 * tcg:128 * tcg + 128], c_idf[0:16, 0:16],
               sem=spe if tcg == NTC - 1 else None)
        n_tr = spe.n
        B.wt("dve", spe, n_tr)
        cp("dve", dcol[:, 0:4], pb[4][:, 0:4], sem=sdv)
        cp("dve", uT[:, 0:64], pb[4][:, 16:80], sem=sdv)
        bank_release(4, sdv)
        for tcg in range(NTC):
            for n in range(NN):
                idx = tcg * NN + n
                ts(diag[:, 128 * idx:128 * idx + 128], c_idb[:],
                   uT[:, idx:idx + 1], None, OP.mult,
                   sem=sdv if idx == NTC * NN - 1 else None)
        n_diag = dve_fence()
        if DBG:
            B.wt("pl", sdv, n_diag)
            for c in range(CHI):
                B.dma("pl", g("ddb"), t_dbg1[128 * c:128 * c + 128, tc0:tc0 + 512],
                      xcs[:, 512 * c:512 * c + 512])
            B.dma("pl", g("ddb"), t_dbg3[0:1, tc0:tc0 + 512], rowap(5))
            B.dma("pl", g("ddb"), t_dbg4[0:16, tc0:tc0 + 512], usb[0:16, 0:512])
            B.wt("pl", g("ddb"))

        def load_abc(qq):
            asl = qq % 2
            if abc_free[asl] is not None:
                s, v = abc_free[asl]
                B.wt("sp", s, v)
                B.wt("pl", s, v)
            B.wt("sp", dwb, n_abig)
            B.wt("pl", dwb, n_abig)
            srcq = t_abig.rearrange("(p n i) -> p n i", p=128, n=NN)[:, :, QW * qq:QW * qq + QW]
            dst = abc[:, 4096 * asl:4096 * asl + 4096].rearrange("p (n i) -> p n i", n=NN)
            B.dma("sp", dab, dst[:, 0:8, :], srcq[:, 0:8, :])
            B.dma("pl", dab, dst[:, 8:16, :], srcq[:, 8:16, :])
            return dab.n

        abc_ready = {0: load_abc(0)}
        for q in range(NQ):
            asl = q % 2
            dab_n = abc_ready[q]
            if q + 1 < NQ:
                abc_ready[q + 1] = load_abc(q + 1)
            for tcg in range(NTC):
                bS = 4 + (tcg % 2)
                bank_acquire(bS, "pe")
                for h2 in range(2):
                    B.wt("act", g("dab"), dab_n)
                    B.wt("act", sfen, n_diag)
                    if ebf_busy[h2] is not None:
                        s, v = ebf_busy[h2]
                        B.wt("act", s, v)
                    act(ebf[:, 2048 * h2:2048 * h2 + 2048],
                        abc[:, 4096 * asl + 2048 * h2:4096 * asl + 2048 * h2 + 2048],
                        AF.Exp, scale=dcol[:, tcg:tcg + 1], sem=sac)
                    B.wt("dve", sac)
                    ts(ebf[:, 2048 * h2:2048 * h2 + 2048],
                       ebf[:, 2048 * h2:2048 * h2 + 2048], 1e-6, 1.0,
                       OP.max, OP.min, sem=sdv)
                    B.wt("pe", sdv)
                    for n8 in range(8):
                        n = 8 * h2 + n8
                        idx = tcg * NN + n
                        mm(pb[bS][:, 0:QW], diag[:, 128 * idx:128 * idx + 128],
                           ebf[:, QW * n:QW * n + QW], n == 0, False,
                           sem=spe if n8 == 7 and h2 == 0 else None)
                    if h2 == 1:
                        mm(pb[bS][:, 0:QW], c_idb[:], dbc_sb[:, QW * q:QW * q + QW],
                           False, True, sem=spe)
                    ebf_busy[h2] = (spe, spe.n)
                if tcg == NTC - 1:
                    abc_free[asl] = (sac, sac.n)
                B.wt("dve", spe)
                ssl = tcg % 2
                cp("dve", stm[:, 0:256], pb[bS][:, 0:QW], sem=sdv, drain=False)
                dve_fence()
                bank_release(bS, sdv)
                bT = 6
                bank_acquire(bT, "pe")
                B.wt("pe", sfen)
                tp(pb[bT][:, 0:128], stm[:, 0:128], c_idf[:])
                tp(pb[bT][:, 128:256], stm[:, 128:256], c_idf[:], sem=spe)
                B.wt("dve", spe)
                cp("dve", sfm[:, 128 * tcg:128 * tcg + 128], pb[bT][:, 0:128])
                cp("dve", sfm[:, 512 + 128 * tcg:512 + 128 * tcg + 128],
                   pb[bT][:, 128:256], sem=sdv, drain=False)
                bank_release(bT, sdv)
            for half in range(2):
                c = 2 * q + half
                zsl = c % 2
                if szsl_free[zsl] is not None:
                    s, v = szsl_free[zsl]
                    B.wt("sp", s, v)
                B.wt("sp", dzo, zsp_written[(t, c)])
                B.dma("sp", dzi, szsl[:, 512 * zsl:512 * zsl + 512],
                      t_zsp[128 * c:128 * c + 128, tc0:tc0 + 512])
                B.wt("dve", dzi)
                B.wt("dve", spe)
                tt(tmp1[:], xcs[:, 512 * c:512 * c + 512], pb[2][:, 0:512], OP.subtract)
                tt(tmp1[:], tmp1[:], pb[3][:, 0:512], OP.mult)
                ts(tmp1[:], tmp1[:], gam_sb[:, c:c + 1], bet_sb[:, c:c + 1],
                   OP.mult, OP.add)
                tt(tmp1[:], tmp1[:], sfm[:, 512 * half:512 * half + 512], OP.mult)
                tt(gbf[:, 512 * c:512 * c + 512], tmp1[:],
                   szsl[:, 512 * zsl:512 * zsl + 512], OP.mult, sem=sdv, drain=False)
                szsl_free[zsl] = (sdv, sdv.n)
                if DBG:
                    B.wt("pl", sdv)
                    B.dma("pl", g("ddb"), t_dbg2[128 * c:128 * c + 128, tc0:tc0 + 512],
                          sfm[:, 512 * half:512 * half + 512])
                    B.wt("pl", g("ddb"))
        n_g = dve_fence()
        bank_release(2, sdv); bank_release(3, sdv)

        bank_acquire(0, "pe"); bank_acquire(1, "pe")
        B.wt("sp", dwb, n_wcast)
        for m in range(CHH):
            sl = m % 2
            if wout_free[sl] is not None:
                s, v = wout_free[sl]
                B.wt("sp", s, v)
            B.dma("sp", dwo, woutb[:, 2048 * sl:2048 * sl + 2048],
                  t_woutb.rearrange("(k p) m -> p k m", p=128)[:, :, 128 * m:128 * m + 128])
            bm = 4 + (m % 2)
            bank_acquire(bm, "pe")
            B.wt("pe", dwo)
            B.wt("pe", sfen, n_g)
            for kk in range(CHI):
                mm(pb[bm][:, 0:512],
                   woutb[:, 2048 * sl + 128 * kk:2048 * sl + 128 * kk + 128],
                   gbf[:, 512 * kk:512 * kk + 512], kk == 0, kk == CHI - 1,
                   sem=spe if kk == CHI - 1 else None)
            wout_free[sl] = (spe, spe.n)
            k, dn = load_x(128 * m, tc0)
            B.wt("dve", dxa, dn)
            B.wt("dve", spe)
            tt(x_pool[:, 512 * k:512 * k + 512], pb[bm][:, 0:512],
               x_pool[:, 512 * k:512 * k + 512], OP.add, sem=sdv)
            bank_release(bm, sdv)
            B.wt("sp", sdv)
            B.dma("sp", dxo, t_xmid[128 * m:128 * m + 128, tc0:tc0 + 512],
                  x_pool[:, 512 * k:512 * k + 512])
            B.wt("dve", spe)
            ln_stats_chunk(x_pool[:, 512 * k:512 * k + 512], m % 2, 0, 1,
                           m == 0, m == CHH - 1)
            xslot_free[k] = [(dxo, dxo.n), (sdv, sdv.n)]
        ln_rows(0, 1, mu_i=0, rstd_i=2, invn_c=C_1024, tmp_i=1)
        B.wt("dve", sac)
        tt(rowap(4), rowap(0), rowap(2), OP.mult)
        ts(rowap(4), rowap(4), -1.0, None, OP.mult, sem=sdv)
        B.wt("sp", sdv)
        B.dma("sp", dst, t_stash[0:1, tc0:tc0 + 512], rowap(2))
        B.dma("sp", dst, t_stash[1:2, tc0:tc0 + 512], rowap(4))
        bcast_rows(rowap(2), 4)
        bcast_rows(rowap(4), 5)
        n_bc1 = spe.n
        for m in range(CHH):
            B.wt("sp", dxo)
            k, dn = load_x(128 * m, tc0, src=t_xmid, dsem=dxi)
            if DBG:
                B.wt("pl", dxi, dn)
                B.dma("pl", g("ddb"), t_dbg6[128 * m:128 * m + 128, tc0:tc0 + 512],
                      x_pool[:, 512 * k:512 * k + 512])
            B.wt("dve", dxi, dn)
            B.wt("dve", spe, n_bc1)
            tt(tmp1[:], x_pool[:, 512 * k:512 * k + 512], pb[4][:, 0:512], OP.mult)
            tt(tmp1[:], tmp1[:], pb[5][:, 0:512], OP.add, sem=sdv)
            B.wt("dve", sdv)
            red(mx1c[:, 8 * t + m:8 * t + m + 1], tmp1[:], OP.max, absval=True, sem=sdv)
            xslot_free[k] = [(sdv, sdv.n)]
        bank_release(4, sdv); bank_release(5, sdv)

    dcc, cc1s = g("dcc"), g("cc1")
    B.wt("dve", sdv)
    red(mxp[:, 0:1], mx1c[:], OP.max, absval=False, sem=sdv)
    n_mx = dve_fence()
    bank_acquire(6, "pe")
    B.wt("pe", sfen, n_mx)
    tp(pb[6][0:1, 0:128], mxp[:, 0:1], c_idf[:], sem=spe)
    B.wt("dve", spe)
    cp("dve", rowap(3)[0:1, 0:128], pb[6][0:1, 0:128], sem=sdv)
    bank_release(6, sdv)
    B.wt("dve", sdv)
    red(rowap(3)[0:1, 128:129], rowap(3)[0:1, 0:128], OP.max, absval=False, sem=sdv)
    B.wt("sp", sdv)
    B.dma("sp", dcc, cc1_in, rowap(3)[0:1, 128:129])
    B.wt("pl", dcc)
    cc("AllReduce", OP.max, [cc1_in], [cc1_out], cc1s)
    B.wt("sp", cc1s, 1)
    B.dma("sp", dcc, rowap(3)[0:1, 129:130], cc1_out)
    B.wt("dve", dcc)
    tt(rowap(3)[0:1, 130:131], rowap(3)[0:1, 129:130], scal[0:1, 2:3], OP.mult)
    ts(rowap(3)[0:1, 130:131], rowap(3)[0:1, 130:131], cst(C_R127), None, OP.mult, sem=sdv)
    B.wt("dve", sdv)
    mset("dve", rowap(3)[0:1, 136:144], 1.0, sem=sdv)
    B.wt("dve", sdv)
    cp("dve", rowap(3)[0:1, 136:137], rowap(3)[0:1, 130:131], sem=sdv)
    B.wt("dve", sdv)
    B.op("dve", lambda E: E.reciprocal(rowap(3)[0:1, 144:152], rowap(3)[0:1, 136:144]), sem=sdv)
    n_rx = dve_fence()
    bank_acquire(6, "pe")
    B.wt("pe", sfen, n_rx)
    mm(pb[6][:, 0:1], c_ones1[:], rowap(3)[0:1, 130:131], True, True)
    mm(pb[6][:, 1:2], c_ones1[:], rowap(3)[0:1, 144:145], True, True, sem=spe)
    B.wt("dve", spe)
    cp("dve", scal[:, 4:5], pb[6][:, 0:1], sem=sdv)
    cp("dve", scal[:, 5:6], pb[6][:, 1:2], sem=sdv)
    bank_release(6, sdv)
    n_sx = sdv.n

    for snm in ("dxa", "dwi", "dwo", "dab", "dzo", "dzi", "dxo", "dxi", "dst"):
        for e in ENGS:
            B.wt(e, g(snm))
    esB.close()
    esC = ExitStack()
    sbC = lambda name, shape, dt: esC.enter_context(nc.sbuf_tensor(name, shape, dt))
    xq = sbC("xq", [128, CHH * 512], BF16)
    qv = sbC("qv", [128, 512], F32)
    qi = sbC("qi", [128, 512], I32)
    qf = sbC("qf", [128, 512], F32)
    qg = sbC("qg", [128, 512], F32)
    glb = sbC("glb", [128, 2 * CHH * 128], BF16)
    ulb = sbC("ulb", [128, 2 * CHH * 128], BF16)
    sgt = sbC("sgt", [128, 512], F32)
    hbf = sbC("hbf", [128, 2 * 512], F32)
    mxh = sbC("mxh", [128, 24], F32)
    hqt = sbC("hqt", [128, CHF * 512], BF16)
    hin = sbC("hin", [128, 2 * 512], F32)
    dlb = sbC("dlb", [128, 2 * CHF * 128], BF16)
    yst = sbC("yst", [128, 2 * 512], F32)
    xm_pool = sbC("xm_pool", [128, 2 * 512], F32)
    ln1r = sbC("ln1r", [1, 2 * 512], F32)

    dho, dhi, dgl2, dy = g("dho"), g("dhi"), g("dgl"), g("dy")
    emit_prep(len(prep_units))
    n_prep_dgl = dgl.n
    mset("dve", mxh[:], 0.0)
    mset("dve", mxp[:, 1:2], 0.0, sem=sdv)
    B.wt("dve", sdv)

    gl_free = [None, None]
    ul_free = [None, None]
    hb_free = [None, None]
    xm_free = [None, None]
    hin_free = [None, None]
    dl_free = [None, None]
    yst_free = [None, None]

    for t in range(NT):
        tc0 = t * T
        B.dma("sp", dst, ln1r[0:1, 0:512], t_stash[0:1, tc0:tc0 + 512])
        B.dma("sp", dst, ln1r[0:1, 512:1024], t_stash[1:2, tc0:tc0 + 512])
        nst = dst.n
        bank_acquire(4, "pe"); bank_acquire(5, "pe")
        B.wt("pe", dst, nst)
        mm(pb[4][:, 0:512], c_ones1[:], ln1r[0:1, 0:512], True, True)
        mm(pb[5][:, 0:512], c_ones1[:], ln1r[0:1, 512:1024], True, True, sem=spe)
        n_bb = spe.n
        for m in range(CHH):
            sl = m % 2
            if xm_free[sl] is not None:
                s, v = xm_free[sl]
                B.wt("sp", s, v)
            B.dma("sp", dxi, xm_pool[:, 512 * sl:512 * sl + 512],
                  t_xmid[128 * m:128 * m + 128, tc0:tc0 + 512])
            B.wt("dve", dxi)
            B.wt("dve", spe, n_bb)
            B.wt("dve", sdv, n_sx)
            tt(qv[:], xm_pool[:, 512 * sl:512 * sl + 512], pb[4][:, 0:512], OP.mult)
            tt(qv[:], qv[:], pb[5][:, 0:512], OP.add, sem=sdv, drain=False)
            xm_free[sl] = (sdv, sdv.n)
            ts(qv[:], qv[:], scal[:, 5:6], cstp(C_HALF), OP.mult, OP.add)
            cp("dve", qi[:], qv[:])
            cp("dve", qf[:], qi[:])
            tt(qg[:], qf[:], qv[:], OP.is_gt)
            tt(qf[:], qf[:], qg[:], OP.subtract)
            ts(qf[:], qf[:], cstp(C_N128), 127.0, OP.max, OP.min)
            ts(xq[:, 512 * m:512 * m + 512], qf[:], scal[:, 4:5], None, OP.mult)
        n_xq = dve_fence()
        bank_release(4, sdv); bank_release(5, sdv)
        for m in range(CHF):
            sl = m % 2
            for (buf, free, dram) in ((glb, gl_free, t_glhs), (ulb, ul_free, t_ulhs)):
                if free[sl] is not None:
                    s, v = free[sl]
                    B.wt("pl", s, v)
                B.wt("pl", dgl, n_prep_dgl)
                B.dma("pl", dgl2, buf[:, 1024 * sl:1024 * sl + 1024],
                      dram.rearrange("k p m -> p k m")[:, :, 128 * m:128 * m + 128])
            ndgl = dgl2.n
            bg = 4 + (m % 2)
            bu = 6 + (m % 2)
            bank_acquire(bg, "pe"); bank_acquire(bu, "pe")
            B.wt("pe", dgl2, ndgl)
            B.wt("pe", sfen, n_xq)
            for kk in range(CHH):
                mm(pb[bg][:, 0:512], glb[:, 1024 * sl + 128 * kk:1024 * sl + 128 * kk + 128],
                   xq[:, 512 * kk:512 * kk + 512], kk == 0, kk == CHH - 1)
            for kk in range(CHH):
                mm(pb[bu][:, 0:512], ulb[:, 1024 * sl + 128 * kk:1024 * sl + 128 * kk + 128],
                   xq[:, 512 * kk:512 * kk + 512], kk == 0, kk == CHH - 1,
                   sem=spe if kk == CHH - 1 else None)
            gl_free[sl] = (spe, spe.n)
            ul_free[sl] = (spe, spe.n)
            B.wt("act", spe)
            B.wt("act", sdv)
            act(sgt[:], pb[bg][:, 0:512], AF.Silu, sem=sac)
            bank_release(bg, sac)
            hsl = m % 2
            if hb_free[hsl] is not None:
                s, v = hb_free[hsl]
                B.wt("dve", s, v)
            B.wt("dve", sac)
            tt(hbf[:, 512 * hsl:512 * hsl + 512], sgt[:], pb[bu][:, 0:512],
               OP.mult, sem=sdv)
            bank_release(bu, sdv)
            B.wt("dve", sdv)
            red(mxh[:, m:m + 1], hbf[:, 512 * hsl:512 * hsl + 512], OP.max,
                absval=True, sem=sdv)
            B.wt("pl", sdv)
            B.dma("pl", dho, t_hd[128 * m:128 * m + 128, tc0:tc0 + 512],
                  hbf[:, 512 * hsl:512 * hsl + 512])
            hb_free[hsl] = (dho, dho.n)
        B.wt("dve", sdv)
        red(mxp[:, 2:3], mxh[:], OP.max, absval=False, sem=sdv)
        B.wt("dve", sdv)
        tt(mxp[:, 1:2], mxp[:, 1:2], mxp[:, 2:3], OP.max, sem=sdv)
    dve_fence()

    cc2s = g("cc2")
    bank_acquire(6, "pe")
    B.wt("pe", sdv)
    B.wt("pe", sfen)
    tp(pb[6][0:1, 0:128], mxp[:, 1:2], c_idf[:], sem=spe)
    B.wt("dve", spe)
    cp("dve", ln1r[0:1, 0:128], pb[6][0:1, 0:128], sem=sdv)
    bank_release(6, sdv)
    B.wt("dve", sdv)
    red(ln1r[0:1, 128:129], ln1r[0:1, 0:128], OP.max, absval=False, sem=sdv)
    B.wt("sp", sdv)
    B.dma("sp", dcc, cc2_in, ln1r[0:1, 128:129])
    B.wt("pl", dcc)
    cc("AllReduce", OP.max, [cc2_in], [cc2_out], cc2s)
    B.wt("sp", cc2s, 1)
    B.dma("sp", dcc, ln1r[0:1, 129:130], cc2_out)
    B.wt("dve", dcc)
    tt(ln1r[0:1, 130:131], ln1r[0:1, 129:130], scal[0:1, 3:4], OP.mult)
    ts(ln1r[0:1, 130:131], ln1r[0:1, 130:131], cst(C_R127), None, OP.mult, sem=sdv)
    B.wt("dve", sdv)
    mset("dve", ln1r[0:1, 136:144], 1.0, sem=sdv)
    B.wt("dve", sdv)
    cp("dve", ln1r[0:1, 136:137], ln1r[0:1, 130:131], sem=sdv)
    B.wt("dve", sdv)
    B.op("dve", lambda E: E.reciprocal(ln1r[0:1, 144:152], ln1r[0:1, 136:144]), sem=sdv)
    n_rh = dve_fence()
    bank_acquire(6, "pe")
    B.wt("pe", sfen, n_rh)
    mm(pb[6][:, 0:1], c_ones1[:], ln1r[0:1, 130:131], True, True)
    mm(pb[6][:, 1:2], c_ones1[:], ln1r[0:1, 144:145], True, True, sem=spe)
    B.wt("dve", spe)
    cp("dve", scal[:, 6:7], pb[6][:, 0:1], sem=sdv)
    cp("dve", scal[:, 7:8], pb[6][:, 1:2], sem=sdv)
    bank_release(6, sdv)
    n_sh = sdv.n
    if DBG:
        B.wt("pl", sdv, n_sh)
        B.dma("pl", g("ddb"), t_dbg7, scal[:])
        B.wt("pl", g("ddb"))

    for t in range(NT):
        tc0 = t * T
        for kk in range(CHF):
            sl = kk % 2
            if hin_free[sl] is not None:
                s, v = hin_free[sl]
                B.wt("sp", s, v)
            B.wt("sp", dho)
            B.dma("sp", dhi, hin[:, 512 * sl:512 * sl + 512],
                  t_hd[128 * kk:128 * kk + 128, tc0:tc0 + 512])
            B.wt("dve", dhi)
            B.wt("dve", sdv, n_sh)
            ts(qv[:], hin[:, 512 * sl:512 * sl + 512], scal[:, 7:8], cstp(C_HALF),
               OP.mult, OP.add, sem=sdv, drain=False)
            hin_free[sl] = (sdv, sdv.n)
            cp("dve", qi[:], qv[:])
            cp("dve", qf[:], qi[:])
            tt(qg[:], qf[:], qv[:], OP.is_gt)
            tt(qf[:], qf[:], qg[:], OP.subtract)
            ts(hqt[:, 512 * kk:512 * kk + 512], qf[:], cstp(C_N128), 127.0,
               OP.max, OP.min)
        n_hq = dve_fence()
        for m in range(CHH):
            sl = m % 2
            if dl_free[sl] is not None:
                s, v = dl_free[sl]
                B.wt("pl", s, v)
            B.wt("pl", dgl, n_prep_dgl)
            B.dma("pl", dgl2, dlb[:, 2816 * sl:2816 * sl + 2816],
                  t_dlhs.rearrange("k p m -> p k m")[:, :, 128 * m:128 * m + 128])
            ndl = dgl2.n
            bm = 4 + (m % 2)
            bank_acquire(bm, "pe")
            B.wt("pe", dgl2, ndl)
            B.wt("pe", sfen, n_hq)
            for kk in range(CHF):
                mm(pb[bm][:, 0:512],
                   dlb[:, 2816 * sl + 128 * kk:2816 * sl + 128 * kk + 128],
                   hqt[:, 512 * kk:512 * kk + 512], kk == 0, kk == CHF - 1,
                   sem=spe if kk == CHF - 1 else None)
            dl_free[sl] = (spe, spe.n)
            ysl = m % 2
            if yst_free[ysl] is not None:
                s, v = yst_free[ysl]
                B.wt("sp", s, v)
            if xm_free[ysl] is not None:
                s, v = xm_free[ysl]
                B.wt("sp", s, v)
            B.dma("sp", dxi, xm_pool[:, 512 * ysl:512 * ysl + 512],
                  t_xmid[128 * m:128 * m + 128, tc0:tc0 + 512])
            B.wt("dve", dxi)
            B.wt("dve", spe)
            stt(yst[:, 512 * ysl:512 * ysl + 512], pb[bm][:, 0:512],
                scal[:, 6:7], xm_pool[:, 512 * ysl:512 * ysl + 512],
                OP.mult, OP.add, sem=sdv)
            bank_release(bm, sdv)
            xm_free[ysl] = (sdv, sdv.n)
            B.wt("sp", sdv)
            B.dma("sp", dy, d_y[128 * m:128 * m + 128, tc0:tc0 + 512],
                  yst[:, 512 * ysl:512 * ysl + 512])
            yst_free[ysl] = (dy, dy.n)
    B.wt("sp", dy, dy.n)
    B.wt("pl", dy, dy.n)

    with nc.Block() as block:
        @block.sync
        def _(E):
            for f in B.em["sp"]:
                f(E)

        @block.tensor
        def _(E):
            for f in B.em["pe"]:
                f(E)

        @block.vector
        def _(E):
            for f in B.em["dve"]:
                f(E)

        @block.scalar
        def _(E):
            for f in B.em["act"]:
                f(E)

        @block.gpsimd
        def _(E):
            for f in B.em["pl"]:
                f(E)

    esC.close()
    es.close()
    return nc


def kernel(L0_attn_x, L0_attn_W_in, L0_attn_conv_w, L0_attn_ln_gamma,
           L0_attn_ln_beta, L0_attn_W_xproj, L0_attn_A_param, L0_attn_D,
           L0_attn_W_out, L0_ffn_clip_x, L0_ffn_clip_down, L0_ffn_w_gate,
           L0_ffn_w_up, L0_ffn_w_down):
    xf = np.asarray(L0_attn_x, np.float32)
    w_in = np.ascontiguousarray(np.asarray(L0_attn_W_in, np.float32))
    conv_w = np.asarray(L0_attn_conv_w, np.float32).reshape(CI, 4)
    conv_t = np.ascontiguousarray(conv_w.reshape(CHI, 128, 4).transpose(1, 0, 2).reshape(128, CHI * 4))
    gam_t = np.ascontiguousarray(np.asarray(L0_attn_ln_gamma, np.float32).reshape(CHI, 128).T)
    bet_t = np.ascontiguousarray(np.asarray(L0_attn_ln_beta, np.float32).reshape(CHI, 128).T)
    wxp = np.asarray(L0_attn_W_xproj, np.float32)
    wxp_pad = np.zeros((CI, 80), np.float32)
    wxp_pad[:, 0:1] = wxp[:, 0:1]
    wxp_pad[:, 32:48] = wxp[:, 1:17]
    wxp_pad[:, 64:80] = wxp[:, 17:33]
    wxp_t = np.ascontiguousarray(wxp_pad.reshape(CHI, 128, 80).transpose(1, 0, 2).reshape(128, CHI * 80))
    a_t = np.ascontiguousarray(np.asarray(L0_attn_A_param, np.float32).T)
    d_row = np.ascontiguousarray(np.asarray(L0_attn_D, np.float32).reshape(1, CI))
    w_out = np.ascontiguousarray(np.asarray(L0_attn_W_out, np.float32))
    w_gate = np.ascontiguousarray(np.asarray(L0_ffn_w_gate, np.float32))
    w_up = np.ascontiguousarray(np.asarray(L0_ffn_w_up, np.float32))
    w_down = np.ascontiguousarray(np.asarray(L0_ffn_w_down, np.float32))
    cx = np.asarray(L0_ffn_clip_x, np.float32).reshape(1, 1)
    cd = np.asarray(L0_ffn_clip_down, np.float32).reshape(1, 1)
    idf = np.eye(128, dtype=np.float32)
    idb = idf.astype(ml_dtypes.bfloat16)
    ones1 = np.ones((1, 128), np.float32)
    consts = np.broadcast_to(np.asarray(CONSTVALS, np.float32), (128, len(CONSTVALS))).copy()

    nc = build_program()
    in_maps = []
    for b in range(8):
        m = {
            "x": np.ascontiguousarray(xf[b].T),
            "w_in": w_in, "conv_w": conv_t, "gamma_t": gam_t, "beta_t": bet_t,
            "d_row": d_row, "wxp_t": wxp_t, "a_t": a_t, "w_out": w_out,
            "clip_x": cx, "clip_dn": cd,
            "w_gate": w_gate, "w_up": w_up, "w_down": w_down,
            "idf": idf, "idb": idb, "ones1": ones1,
            "consts": consts,
        }
        in_maps.append(m)
    res = run_bass_kernel_spmd(nc, in_maps, core_ids=list(range(8)))
    out = np.stack([np.asarray(res.results[b]["y"], np.float32).T for b in range(8)])
    return out


# version 18
# speedup vs baseline: 1.2653x; 1.1820x over previous
import os
import numpy as np
import ml_dtypes
import concourse.bass as bass
import concourse.mybir as mybir
from concourse.bass_utils import run_bass_kernel_spmd
from contextlib import ExitStack

F32 = mybir.dt.float32
F32R = mybir.dt.float32r
BF16 = mybir.dt.bfloat16
I32 = mybir.dt.int32
AF = mybir.ActivationFunctionType
OP = mybir.AluOpType
AX = mybir.AxisListType.X

S = 2048
H = 1024
CI = 2048
IF = 2816
NT = 4
T = 512
NTC = 4
NQ = 8
QW = 256
NN = 16
CHH = H // 128
CHI = CI // 128
CHF = IF // 128

C_EPS, C_NH, C_1024, C_2048, C_127, C_R127, C_HALF, C_NEG1, C_ONE, C_N128 = range(10)
CONSTVALS = [1e-5, -0.5, 1.0 / 1024.0, 1.0 / 2048.0, 127.0, 1.0 / 127.0, 0.5, -1.0, 1.0, -128.0]

ENGS = ("sp", "pe", "dve", "act", "pl")


class Sem:
    def __init__(self, h):
        self.h = h
        self.n = 0


class Builder:
    def __init__(self, nc):
        self.nc = nc
        self.em = {e: [] for e in ENGS}

    def on(self, eng, fn):
        self.em[eng].append(fn)

    def wt(self, eng, sem, v=None):
        v = sem.n if v is None else v
        if v <= 0:
            return
        self.on(eng, lambda E, h=sem.h, vv=v: E.wait_ge(h, vv))

    def dma(self, eng, sem, out, in_):
        self.on(eng, lambda E, o=out, i=in_, h=sem.h: E.dma_start(out=o, in_=i).then_inc(h, 16))
        sem.n += 16
        return sem.n

    def op(self, eng, fn, sem=None, drain=True):
        if sem is None:
            self.on(eng, fn)
            return None
        sem.n += 1
        if eng in ("dve", "act"):
            self.on(eng, fn)
            self.on(eng, lambda E, h=sem.h: E.drain().then_inc(h, 1))
        else:
            self.on(eng, lambda E, f=fn, h=sem.h: f(E).then_inc(h, 1))
        return sem.n


def build_program():
    nc = bass.Bass("TRN2", target_bir_lowering=False, detect_race_conditions=False)
    B = Builder(nc)

    di = {}
    def d_in(name, shape, dt=F32):
        di[name] = nc.dram_tensor(name, shape, dt, kind="ExternalInput").ap()
        return di[name]

    d_x = d_in("x", [H, S])
    d_win = d_in("w_in", [H, 4096])
    d_cw = d_in("conv_w", [128, CHI * 4])
    d_gam = d_in("gamma_t", [128, CHI])
    d_bet = d_in("beta_t", [128, CHI])
    d_drow = d_in("d_row", [1, CI])
    d_wxp = d_in("wxp_t", [128, CHI * 80])
    d_at = d_in("a_t", [16, CI])
    d_wout = d_in("w_out", [CI, H])
    d_cx = d_in("clip_x", [1, 1])
    d_cd = d_in("clip_dn", [1, 1])
    d_wg = d_in("w_gate", [IF, H])
    d_wu = d_in("w_up", [IF, H])
    d_wd = d_in("w_down", [H, IF])
    d_idf = d_in("idf", [128, 128])
    d_idb = d_in("idb", [128, 128], BF16)
    d_ones1 = d_in("ones1", [1, 128])
    d_cst = d_in("consts", [128, len(CONSTVALS)])
    d_y = nc.dram_tensor("y", [H, S], F32, kind="ExternalOutput").ap()

    DBG = bool(os.environ.get("KDBG"))
    dbk = dict(kind="ExternalOutput") if DBG else {}
    t_winb = nc.dram_tensor("winb_d", [H, 4096], BF16).ap()
    t_woutb = nc.dram_tensor("woutb_d", [CI, H], BF16).ap()
    t_abig = nc.dram_tensor("abig_d", [128 * NN * CI], F32).ap()
    t_zsp = nc.dram_tensor("zsp_d", [CI, S], F32).ap()
    t_xmid = nc.dram_tensor("xmid_d", [H, S], F32).ap()
    t_hd = nc.dram_tensor("h_d", [IF, S], F32, **(dict(kind="ExternalOutput") if bool(os.environ.get("KDBG")) else {})).ap()
    t_stash = nc.dram_tensor("stash_d", [2, S], F32).ap()
    t_glhs = nc.dram_tensor("glhs_d", [CHH, 128, IF], BF16).ap()
    t_ulhs = nc.dram_tensor("ulhs_d", [CHH, 128, IF], BF16).ap()
    t_dlhs = nc.dram_tensor("dlhs_d", [CHF, 128, H], BF16).ap()
    t_dbg1 = nc.dram_tensor("dbg1", [CI, S], F32, **dbk).ap()
    t_dbg2 = nc.dram_tensor("dbg2", [CI, S], F32, **dbk).ap()
    t_dbg3 = nc.dram_tensor("dbg3", [8, S], F32, **dbk).ap()
    t_dbg4 = nc.dram_tensor("dbg4", [16, S], F32, **dbk).ap()
    t_dbg5 = nc.dram_tensor("dbg5", [CI, S], F32, **dbk).ap()
    t_dbg6 = nc.dram_tensor("dbg6", [H, S], F32, **dbk).ap()
    t_dbg7 = nc.dram_tensor("dbg7", [128, 16], F32, **dbk).ap()
    cc1_in = nc.dram_tensor("cc1_in", [1, 1], F32).ap()
    cc1_out = nc.dram_tensor("cc1_out", [1, 1], F32).ap()
    cc2_in = nc.dram_tensor("cc2_in", [1, 1], F32).ap()
    cc2_out = nc.dram_tensor("cc2_out", [1, 1], F32).ap()

    es = ExitStack()
    sb = lambda name, shape, dt: es.enter_context(nc.sbuf_tensor(name, shape, dt))

    c_idf = sb("c_idf", [128, 128], F32)
    c_idb = sb("c_idb", [128, 128], BF16)
    c_ones1 = sb("c_ones1", [1, 128], F32)
    c_ones1r = sb("c_ones1r", [1, 128], F32R)
    c_onescolr = sb("c_onescolr", [128, 1], F32R)
    c_cst = sb("c_cst", [128, len(CONSTVALS)], F32)
    cw_sb = sb("cw_sb", [128, CHI * 4], F32)
    gam_sb = sb("gam_sb", [128, CHI], F32)
    bet_sb = sb("bet_sb", [128, CHI], F32)
    wxp_sb = sb("wxp_sb", [128, CHI * 80], F32)
    wxpW = sb("wxpW", [128, CHI * 80], F32R)
    dbc_sb = sb("dbc_sb", [128, CI], BF16)
    rows = sb("rows", [1, 6 * 512], F32)
    rows_r = sb("rows_r", [1, 1280], F32R)
    scal = sb("scal", [128, 16], F32)
    mx1c = sb("mx1c", [128, 32], F32)
    mxp = sb("mxp", [128, 8], F32)
    pw = sb("pw", [128, 1408], F32)
    pt = sb("pt", [128, 512], F32)
    pi = sb("pi", [128, 512], I32)
    pf = sb("pf", [128, 512], F32)
    pq = sb("pq", [128, 1408], BF16)
    pg_v = pi.bitcast(F32)
    pev = sb("pev", [128, 1024], BF16)
    prow = sb("prow", [128, 24], F32)

    pb = [es.enter_context(nc.psum_tensor(f"pb{i}", [128, 512], F32)) for i in range(8)]

    semnames = ["dxa", "dwi", "dwo", "dab", "dzo", "dzi", "dxo", "dxi", "dst",
                "dho", "dhi", "dgl", "dpw", "dy", "dmc", "dwb", "dcc", "ddb",
                "spe", "sdv", "sac", "spl", "sfen", "cc1", "cc2", "cc3"]
    SM = {n: Sem(es.enter_context(nc.semaphore(n))) for n in semnames}
    g = lambda n: SM[n]
    spe, sdv, sac, spl = g("spe"), g("sdv"), g("sac"), g("spl")

    esB = ExitStack()
    sbB = lambda name, shape, dt: esB.enter_context(nc.sbuf_tensor(name, shape, dt))
    x_pool = sbB("x_pool", [128, 3 * 512], F32)
    ln0bf = sbB("ln0bf", [128, CHH * 512], BF16)
    xr_sl = sbB("xr_sl", [128, 512], F32R)
    sq_sl = sbB("sq_sl", [128, 512], F32R)
    winb = sbB("winb", [128, 2 * CHH * 128], BF16)
    xcraw = sbB("xcraw", [128, 3 * 516], BF16)
    tails = sbB("tails", [128, CHI * 4], F32)
    cvd = sbB("cvd", [128, 8 * 128], BF16)
    xcs = sbB("xcs", [128, CHI * 512], F32)
    xcsr = sbB("xcsr", [128, 2 * 512], F32R)
    szsl = sbB("szsl", [128, 2 * 512], F32)
    xpsb = sbB("xpsb", [128, 512], F32)
    usb = None
    dcol = sbB("dcol", [128, 8], F32)
    uT = sbB("uT", [128, 64], F32)
    diag = sbB("diag", [128, 64 * 128], BF16)
    abc = sbB("abc", [128, 2 * 4096], F32)
    ebf = sbB("ebf", [128, 4096], BF16)
    stm = sbB("stm", [128, 256], F32)
    sfm = sbB("sfm", [128, 2 * 512], F32)
    gbf = sbB("gbf", [128, CHI * 512], BF16)
    woutb = sbB("woutb", [128, 2 * CHI * 128], BF16)
    tmp1 = sbB("tmp1", [128, 512], F32)
    usb = tmp1

    sfen = g("sfen")

    def dve_fence():
        sfen.n += 1
        B.on("dve", lambda E, h=sfen.h: E.drain().then_inc(h, 1))
        return sfen.n

    def mm(out, lhsT, rhs, start, stop, sem=None):
        return B.op("pe", lambda E, o=out, l=lhsT, r=rhs, a=start, b=stop:
                    E.matmul(o, l, r, start=a, stop=b), sem)

    def tp(out, in_, ident, sem=None):
        return B.op("pe", lambda E, o=out, i=in_, d=ident: E.transpose(o, i, d), sem)

    def cp(eng, out, in_, sem=None, drain=True):
        return B.op(eng, lambda E, o=out, i=in_: E.tensor_copy(o, i), sem, drain)

    def tt(out, a, b, o, sem=None, eng="dve", drain=True):
        return B.op(eng, lambda E, x=out, p=a, q=b, r=o: E.tensor_tensor(x, p, q, r), sem, drain)

    def ts(out, in_, s1, s2, o1, o2=None, sem=None, eng="dve", drain=True):
        if s2 is None:
            return B.op(eng, lambda E, x=out, i=in_, a=s1, p=o1:
                        E.tensor_scalar(x, i, a, None, p), sem, drain)
        return B.op(eng, lambda E, x=out, i=in_, a=s1, b=s2, p=o1, q=o2:
                    E.tensor_scalar(x, i, a, b, p, q), sem, drain)

    def stt(out, in0, s, in1, o1, o2, sem=None, eng="dve", drain=True):
        return B.op(eng, lambda E, x=out, i=in0, a=s, j=in1, p=o1, q=o2:
                    E.scalar_tensor_tensor(x, i, a, j, p, q), sem, drain)

    def act(out, in_, fn, bias=0.0, scale=1.0, accum=None, sem=None):
        return B.op("act", lambda E, o=out, i=in_, f=fn, b=bias, s=scale, a=accum:
                    E.activation(o, i, f, bias=b, scale=s, accum_out=a), sem)

    def red(out, in_, o, absval=False, sem=None, drain=True):
        return B.op("dve", lambda E, x=out, i=in_, p=o, a=absval:
                    E.tensor_reduce(x, i, AX, p, apply_absolute_value=a), sem, drain)

    def mset(eng, ap_, v, sem=None, drain=True):
        return B.op(eng, lambda E, a=ap_, vv=v: E.memset(a, vv), sem, drain)

    def cc(kind, op_, ins, outs, sem):
        sem.n += 1
        B.on("pl", lambda E, k=kind, o=op_, i=ins, u=outs, h=sem.h:
             E.collective_compute(k, o, replica_groups=[list(range(8))],
                                  ins=i, outs=u).then_inc(h, 1))
        return sem.n

    bank_free = [None] * 8

    def bank_acquire(i, eng):
        if bank_free[i] is not None:
            s, v = bank_free[i]
            B.wt(eng, s, v)
            bank_free[i] = None

    def bank_release(i, sem):
        bank_free[i] = (sem, sem.n)

    rowap = lambda i: rows[0:1, 512 * i:512 * i + 512]
    row1 = lambda i, w=1: rows[0:1, 512 * i:512 * i + w]
    cst = lambda c: c_cst[0:1, c:c + 1]
    cstp = lambda c: c_cst[:, c:c + 1]

    dmc, dwb = g("dmc"), g("dwb")
    for out, src in [(c_idf[:], d_idf), (c_idb[:], d_idb), (c_ones1[:], d_ones1),
                     (c_cst[:], d_cst), (cw_sb[:], d_cw), (gam_sb[:], d_gam),
                     (bet_sb[:], d_bet), (wxp_sb[:], d_wxp)]:
        B.dma("sp", dmc, out, src)
    B.dma("sp", dmc, scal[0:1, 0:1], d_cx)
    B.dma("sp", dmc, scal[0:1, 1:2], d_cd)
    B.dma("sp", dmc, xcs[0:16, 0:CI], d_at)

    n_dmc0 = dmc.n
    B.wt("pe", dmc, n_dmc0)
    B.wt("act", dmc, n_dmc0)
    act(scal[0:1, 2:3], scal[0:1, 0:1], AF.Sigmoid)
    act(scal[0:1, 3:4], scal[0:1, 1:2], AF.Sigmoid, sem=sac)

    B.wt("dve", dmc, n_dmc0)
    cp("dve", c_ones1r[:], c_ones1[:])
    cp("dve", c_onescolr[:], c_cst[:, C_ONE:C_ONE + 1])
    mset("dve", rowap(0), 1.0, sem=sdv)
    B.wt("dve", sdv)
    cp("dve", rows_r[0:1, 0:512], rowap(0), sem=sdv)
    n_ones_r = sdv.n

    at = xcs[0:16, 0:CI]
    a_abs = xcs[32:48, 0:CI]
    a_rel = xcs[32:48, CI:2 * CI]
    B.wt("act", sdv, n_ones_r)
    act(a_abs, at, AF.Abs, sem=sac)
    B.wt("act", sac)
    act(a_abs, a_abs, AF.Exp, scale=c_cst[0:16, C_NEG1:C_NEG1 + 1], sem=sac)
    B.wt("act", sac)
    act(a_abs, a_abs, AF.Ln, bias=c_cst[0:16, C_ONE:C_ONE + 1], sem=sac)
    B.wt("act", sac)
    act(a_rel, at, AF.Relu, sem=sac)
    n_aact = sac.n
    B.wt("dve", sac, n_aact)
    tt(a_abs, a_abs, a_rel, OP.add)
    ts(a_abs, a_abs, 0.1, 10.0, OP.max, OP.min)
    ts(a_abs, a_abs, -1.0, None, OP.mult, sem=sdv)
    n_aprime = sdv.n
    B.wt("sp", sdv, n_aprime)
    B.dma("sp", dwb, t_abig[0:NN * CI].rearrange("(a b) -> a b", a=16), a_abs)
    nsz = NN * CI
    for _ in range(7):
        B.wt("sp", dwb)
        B.dma("sp", dwb, t_abig[nsz:2 * nsz], t_abig[0:nsz])
        nsz *= 2
    B.wt("sp", dwb)
    n_abig = dwb.n

    B.dma("sp", dmc, xcs[96:97, 0:CI], d_drow)
    n_drow = dmc.n
    B.wt("dve", dmc, n_drow)
    cp("dve", ln0bf[0:1, 0:CI], xcs[96:97, 0:CI], sem=sdv)
    n_drowb = sdv.n
    B.wt("dve", dmc, n_dmc0)
    cp("dve", gbf[0:1, 0:128], c_ones1[:], sem=sdv)
    n_ones1b = sdv.n
    B.wt("pe", sdv, max(n_drowb, n_ones1b))
    for kb in range(4):
        bank_acquire(kb, "pe")
    mm(pb[0][:, 0:512], gbf[0:1, 0:128], ln0bf[0:1, 0:512], True, True)
    mm(pb[1][:, 0:512], gbf[0:1, 0:128], ln0bf[0:1, 512:1024], True, True)
    mm(pb[2][:, 0:512], gbf[0:1, 0:128], ln0bf[0:1, 1024:1536], True, True)
    mm(pb[3][:, 0:512], gbf[0:1, 0:128], ln0bf[0:1, 1536:2048], True, True, sem=spe)
    n_dbc = spe.n
    B.wt("dve", spe, n_dbc)
    for k in range(4):
        cp("dve", dbc_sb[:, 512 * k:512 * k + 512], pb[k][:, 0:512])
    B.op("dve", lambda E: E.engine_nop(), sem=sdv)
    n_dbcs = sdv.n
    for k in range(4):
        bank_release(k, sdv)

    B.wt("dve", dmc, n_dmc0)
    for c in range(CHI):
        ts(wxpW[:, 80 * c:80 * c + 80], wxp_sb[:, 80 * c:80 * c + 80],
           gam_sb[:, c:c + 1], None, OP.mult, sem=sdv if c == CHI - 1 else None)
    n_wxpw = sdv.n
    B.wt("pe", sdv, n_wxpw)
    bank_acquire(4, "pe"); bank_acquire(5, "pe")
    for c in range(CHI):
        mm(pb[4][0:80, 0:1], wxp_sb[:, 80 * c:80 * c + 80], gam_sb[:, c:c + 1], c == 0, c == CHI - 1)
    for c in range(CHI):
        mm(pb[5][0:80, 0:1], wxp_sb[:, 80 * c:80 * c + 80], bet_sb[:, c:c + 1],
           c == 0, c == CHI - 1, sem=spe if c == CHI - 1 else None)
    n_cgb = spe.n
    B.wt("dve", spe, n_cgb)
    cp("dve", xpsb[0:80, 0:1], pb[4][0:80, 0:1])
    cp("dve", xpsb[0:80, 1:2], pb[5][0:80, 0:1], sem=sdv)
    n_cgev = sdv.n
    bank_release(4, sdv); bank_release(5, sdv)
    B.wt("pe", sdv, n_cgev)
    bank_acquire(7, "pe")
    tp(pb[7][0:1, 0:80], xpsb[0:80, 0:1], c_idf[0:80, 0:80], sem=spe)
    n_cgt = spe.n
    B.wt("dve", spe, n_cgt)
    cp("dve", rows_r[0:1, 512:592], pb[7][0:1, 0:80])
    cp("dve", scal[0:80, 8:9], xpsb[0:80, 1:2], sem=sdv)
    n_cgr = sdv.n
    bank_release(7, sdv)
    cg_r = rows_r[0:1, 512:592]
    cb_col = scal[0:80, 8:9]
    negmur_r = rows_r[0:1, 672:1184]
    ones512_r = rows_r[0:1, 0:512]

    B.wt("sp", sac)
    B.wt("sp", spe)
    B.wt("dve", spe)
    for k in range(CHH):
        B.wt("sp", sdv)
        B.wt("sp", dwb)
        B.dma("sp", dmc, xcs[:, 0:4096], d_win[128 * k:128 * k + 128, :])
        B.wt("dve", dmc)
        B.wt("dve", dwb)
        cp("dve", ln0bf[:, 0:4096], xcs[:, 0:4096], sem=sdv)
        B.wt("sp", sdv)
        B.dma("sp", dwb, t_winb[128 * k:128 * k + 128, :], ln0bf[:, 0:4096])
    for k in range(CHI):
        B.wt("sp", sdv)
        B.wt("sp", dwb)
        B.dma("sp", dmc, xcs[:, 4096:4096 + 1024], d_wout[128 * k:128 * k + 128, :])
        B.wt("dve", dmc)
        B.wt("dve", dwb)
        cp("dve", ln0bf[:, 0:1024], xcs[:, 4096:4096 + 1024], sem=sdv)
        B.wt("sp", sdv)
        B.dma("sp", dwb, t_woutb[128 * k:128 * k + 128, :], ln0bf[:, 0:1024])
    n_wcast = dwb.n

    mset("dve", tails[:], 0.0)
    mset("dve", mx1c[:], 0.0)
    mset("dve", mxp[:], 1.0, sem=sdv)
    n_zeros = sdv.n

    dpw, dgl = g("dpw"), g("dgl")

    def quant_rows(src_ap, ncols):
        mset("dve", prow[:, 0:8], 1.0, sem=sdv, drain=False)
        B.wt("dve", sdv)
        red(prow[:, 0:1], src_ap, OP.max, absval=True, sem=sdv, drain=False)
        B.wt("dve", sdv)
        ts(prow[:, 0:1], prow[:, 0:1], 1e-30, None, OP.max)
        B.op("dve", lambda E: E.reciprocal(prow[:, 8:16], prow[:, 0:8]), sem=sdv, drain=False)
        B.wt("dve", sdv)
        ts(prow[:, 16:17], prow[:, 8:9], cstp(C_127), None, OP.mult)
        ts(prow[:, 17:18], prow[:, 0:1], cstp(C_R127), None, OP.mult)
        for off in range(0, ncols, 512):
            w5 = min(512, ncols - off)
            ts(pt[:, 0:w5], pw[:, off:off + w5], prow[:, 16:17], cstp(C_HALF), OP.mult, OP.add)
            cp("dve", pi[:, 0:w5], pt[:, 0:w5])
            cp("dve", pf[:, 0:w5], pi[:, 0:w5])
            tt(pg[:, 0:w5], pf[:, 0:w5], pt[:, 0:w5], OP.is_gt)
            tt(pf[:, 0:w5], pf[:, 0:w5], pg[:, 0:w5], OP.subtract)
            ts(pq[:, off:off + w5], pf[:, 0:w5], prow[:, 17:18], None, OP.mult,
               sem=sdv, drain=False)
        dve_fence()

    def _prep_half(w_ap, c, off, w5, out_off_fn, j0):
        B.wt("pl", sdv)
        B.dma("pl", dpw, pw[:, 0:w5], w_ap[128 * c:128 * c + 128, off:off + w5])
        B.wt("dve", dpw)
        B.wt("dve", spe)
        for o2 in range(0, w5, 512):
            w6 = min(512, w5 - o2)
            ts(pt[:, 0:w6], pw[:, o2:o2 + w6], prow[:, 16:17], cstp(C_HALF), OP.mult, OP.add)
            cp("dve", pi[:, 0:w6], pt[:, 0:w6])
            cp("dve", pf[:, 0:w6], pi[:, 0:w6])
            tt(pg_v[:, 0:w6], pf[:, 0:w6], pt[:, 0:w6], OP.is_gt)
            tt(pf[:, 0:w6], pf[:, 0:w6], pg_v[:, 0:w6], OP.subtract)
            ts(pq[:, o2:o2 + w6], pf[:, 0:w6], prow[:, 17:18], None, OP.mult, sem=sdv)
        dve_fence()
        nblk = w5 // 128
        for g8 in range(0, nblk, 8):
            gn = min(8, nblk - g8)
            bidx = 6 + ((g8 // 8) % 2)
            bank_acquire(bidx, "pe")
            B.wt("pe", sfen)
            for j in range(gn):
                tp(pb[bidx].bitcast(BF16)[:, 128 * j:128 * j + 128],
                   pq[:, 128 * (g8 + j):128 * (g8 + j) + 128],
                   c_idb[:], sem=spe if j == gn - 1 else None)
            B.wt("dve", spe)
            B.wt("dve", dgl)
            psl = 0
            cp("dve", pev[:, psl:psl + 128 * gn],
               pb[bidx].bitcast(BF16)[:, 0:128 * gn], sem=sdv)
            bank_release(bidx, sdv)
            B.wt("pl", sdv)
            B.dma("pl", dgl, out_off_fn(c, j0 + g8, gn),
                  pev[:, psl:psl + 128 * gn].rearrange("p (j i) -> p j i", j=gn))

    def emit_prep_unit(w_ap, c, ncols, out_off_fn):
        mset("dve", prow[:, 0:8], 1.0, sem=sdv)
        B.wt("dve", sdv)
        mset("dve", prow[:, 18:19], 0.0, sem=sdv)
        for off in range(0, ncols, 1408):
            w5 = min(1408, ncols - off)
            B.wt("pl", sdv)
            B.dma("pl", dpw, pw[:, 0:w5], w_ap[128 * c:128 * c + 128, off:off + w5])
            B.wt("dve", dpw)
            B.wt("dve", sdv)
            red(prow[:, 1:2], pw[:, 0:w5], OP.max, absval=True, sem=sdv)
            B.wt("dve", sdv)
            tt(prow[:, 18:19], prow[:, 18:19], prow[:, 1:2], OP.max, sem=sdv)
            B.wt("dve", sdv)
        ts(prow[:, 0:1], prow[:, 18:19], 1e-30, None, OP.max, sem=sdv)
        B.wt("dve", sdv)
        B.op("dve", lambda E: E.reciprocal(prow[:, 8:16], prow[:, 0:8]), sem=sdv)
        B.wt("dve", sdv)
        ts(prow[:, 16:17], prow[:, 8:9], cstp(C_127), None, OP.mult)
        ts(prow[:, 17:18], prow[:, 0:1], cstp(C_R127), None, OP.mult, sem=sdv)
        for off in range(0, ncols, 1408):
            w5 = min(1408, ncols - off)
            _prep_half(w_ap, c, off, w5, out_off_fn, off // 128)

    def mk_off(dram, width):
        def f(c, jstart, jn, d=dram, w=width):
            return d[jstart:jstart + jn, :, 128 * c:128 * c + 128].rearrange("j p i -> p j i")
        return f

    prep_units = (
        [(d_wg, c, H, mk_off(t_glhs, IF)) for c in range(CHF)]
        + [(d_wu, c, H, mk_off(t_ulhs, IF)) for c in range(CHF)]
        + [(d_wd, c, IF, mk_off(t_dlhs, H)) for c in range(CHH)]
    )
    prep_pos = [0]

    def emit_prep(k):
        while k > 0 and prep_pos[0] < len(prep_units):
            w_ap, c, ncols, fn = prep_units[prep_pos[0]]
            emit_prep_unit(w_ap, c, ncols, fn)
            prep_pos[0] += 1
            k -= 1

    dxa, dwi, dwo, dab = g("dxa"), g("dwi"), g("dwo"), g("dab")
    dzo, dzi, dxo, dxi, dst = g("dzo"), g("dzi"), g("dxo"), g("dxi"), g("dst")

    xslot_free = [[] for _ in range(3)]
    xload_k = [0]

    def load_x(rows_lo, cols_lo, src=None, dsem=None):
        k = xload_k[0] % 3
        for s, v in xslot_free[k]:
            B.wt("sp", s, v)
        xslot_free[k] = []
        src = d_x if src is None else src
        dsem = dxa if dsem is None else dsem
        B.dma("sp", dsem, x_pool[:, 512 * k:512 * k + 512],
              src[rows_lo:rows_lo + 128, cols_lo:cols_lo + 512])
        xload_k[0] += 1
        return k, dsem.n

    win_free = [None, None]
    wout_free = [None, None]
    abc_free = [None, None]
    ebf_busy = [None, None]
    xcraw_free = [None] * 3
    cvd_free = [None, None]
    conv_fence = {}
    xcsr_free = [None, None]
    szsl_free = [None, None]
    zsp_written = {}

    for e in ENGS:
        B.wt(e, dwb, dwb.n)
        B.wt(e, g("dmc"), g("dmc").n)
    B.wt("dve", sdv, n_zeros)

    def ln_stats_chunk(src_f32, slot2, bsum, bssq, first, last):
        xr = xr_sl[:, 0:512]
        sq = sq_sl[:, 0:512]
        cp("dve", xr, src_f32, sem=sdv, drain=False)
        tt(sq, xr, xr, OP.mult)
        dve_fence()
        B.wt("pe", sfen)
        mm(pb[bsum][0:1, 0:512], c_onescolr[:], xr, first, last)
        mm(pb[bssq][0:1, 0:512], c_onescolr[:], sq, first, last, sem=spe)

    def ln_rows(bsum, bssq, mu_i, rstd_i, invn_c, tmp_i):
        B.wt("dve", spe)
        B.wt("dve", g("dst"))
        ts(rowap(mu_i), pb[bsum][0:1, 0:512], cst(invn_c), None, OP.mult, sem=sdv)
        ts(rowap(tmp_i), pb[bssq][0:1, 0:512], cst(invn_c), None, OP.mult, sem=sdv)
        bank_release(bsum, sdv); bank_release(bssq, sdv)
        tt(rowap(rstd_i), rowap(mu_i), rowap(mu_i), OP.mult)
        tt(rowap(rstd_i), rowap(tmp_i), rowap(rstd_i), OP.subtract, sem=sdv)
        nf = dve_fence()
        B.wt("act", sfen, nf)
        B.wt("act", sdv)
        act(rowap(rstd_i), rowap(rstd_i), AF.Ln, bias=cst(C_EPS), sem=sac)
        B.wt("act", sac)
        act(rowap(rstd_i), rowap(rstd_i), AF.Exp, scale=cst(C_NH), sem=sac)

    def bcast_rows(row_ap, bidx, sem=None, val=None):
        sem = sac if sem is None else sem
        bank_acquire(bidx, "pe")
        B.wt("pe", sem, sem.n if val is None else val)
        B.wt("pe", sfen)
        mm(pb[bidx][:, 0:512], c_ones1[:], row_ap, True, True, sem=spe)

    for t in range(NT):
        tc0 = t * T
        emit_prep(13)
        bank_acquire(0, "pe"); bank_acquire(1, "pe")
        for c in range(CHH):
            k, dn = load_x(128 * c, tc0)
            B.wt("dve", dxa, dn)
            if xcsr_free[c % 2] is not None:
                s, v = xcsr_free[c % 2]
            B.wt("dve", spe)
            ln_stats_chunk(x_pool[:, 512 * k:512 * k + 512], c % 2, 0, 1,
                           c == 0, c == CHH - 1)
            xslot_free[k] = [(sdv, sdv.n)]
        ln_rows(0, 1, mu_i=0, rstd_i=2, invn_c=C_1024, tmp_i=1)
        bcast_rows(rowap(0), 2)
        bcast_rows(rowap(2), 3)
        n_bc0 = spe.n
        for c in range(CHH):
            k, dn = load_x(128 * c, tc0)
            B.wt("dve", dxa, dn)
            B.wt("dve", spe, n_bc0)
            tt(tmp1[:], x_pool[:, 512 * k:512 * k + 512], pb[2][:, 0:512], OP.subtract)
            tt(ln0bf[:, 512 * c:512 * c + 512], tmp1[:], pb[3][:, 0:512], OP.mult,
               sem=sdv, drain=False)
            xslot_free[k] = [(sdv, sdv.n)]
        n_ln0 = dve_fence()
        bank_release(2, sdv); bank_release(3, sdv)

        bank_acquire(0, "pe"); bank_acquire(1, "pe")
        B.wt("sp", dwb, n_wcast)
        def emit_conv_chunk(c):
            rsl = c % 3
            xcr = xcraw[:, 516 * rsl:516 * rsl + 516]
            dsl = 512 * (c % 2)
            bcv = 2 + (c % 2)
            bank_acquire(bcv, "pe")
            B.wt("pe", sfen, conv_fence[c])
            for kk in range(4):
                mm(pb[bcv][:, 0:512], cvd[:, dsl + 128 * kk:dsl + 128 * kk + 128],
                   xcr[:, kk:kk + 512], kk == 0, kk == 3,
                   sem=spe if kk == 3 else None)
            xcraw_free[rsl] = (spe, spe.n)
            cvd_free[c % 2] = (spe, spe.n)
            B.wt("act", spe)
            act(xcs[:, 512 * c:512 * c + 512], pb[bcv][:, 0:512], AF.Silu, sem=sac)
            bank_release(bcv, sac)
            B.wt("dve", sac)
            if xcsr_free[c % 2] is not None:
                s, v = xcsr_free[c % 2]
                B.wt("dve", s, v)
            B.wt("dve", spe)
            xrr = xcsr[:, 512 * (c % 2):512 * (c % 2) + 512]
            cp("dve", xrr, xcs[:, 512 * c:512 * c + 512], sem=sdv, drain=False)
            sq = sq_sl[:, 0:512]
            tt(sq, xrr, xrr, OP.mult)
            dve_fence()
            B.wt("pe", sfen)
            mm(pb[0][0:1, 0:512], c_onescolr[:], xrr, c == 0, c == CHI - 1)
            mm(pb[1][0:1, 0:512], c_onescolr[:], sq, c == 0, c == CHI - 1)
            if c == 0:
                bank_acquire(7, "pe")
            mm(pb[7][0:80, 0:512], wxpW[:, 80 * c:80 * c + 80], xrr,
               c == 0, False, sem=spe)
            xcsr_free[c % 2] = (spe, spe.n)

        for m in range(32):
            sl = m % 2
            if win_free[sl] is not None:
                s, v = win_free[sl]
                B.wt("sp", s, v)
            B.dma("sp", dwi, winb[:, 1024 * sl:1024 * sl + 1024],
                  t_winb.rearrange("(k p) m -> p k m", p=128)[:, :, 128 * m:128 * m + 128])
            bm = 4 + (m % 2)
            bank_acquire(bm, "pe")
            B.wt("pe", dwi)
            B.wt("pe", sfen, n_ln0)
            for kk in range(CHH):
                mm(pb[bm][:, 0:512], winb[:, 1024 * sl + 128 * kk:1024 * sl + 128 * kk + 128],
                   ln0bf[:, 512 * kk:512 * kk + 512], kk == 0, kk == CHH - 1,
                   sem=spe if kk == CHH - 1 else None)
            win_free[sl] = (spe, spe.n)
            if m < CHI:
                c = m
                rsl = c % 3
                if xcraw_free[rsl] is not None:
                    s, v = xcraw_free[rsl]
                    B.wt("dve", s, v)
                if cvd_free[c % 2] is not None:
                    s, v = cvd_free[c % 2]
                    B.wt("dve", s, v)
                B.wt("dve", spe)
                xcr = xcraw[:, 516 * rsl:516 * rsl + 516]
                dsl = 512 * (c % 2)
                cp("dve", xcr[:, 0:3], tails[:, 4 * c:4 * c + 3])
                cp("dve", xcr[:, 3:515], pb[bm][:, 0:512], sem=sdv, drain=False)
                B.wt("dve", sdv)
                cp("dve", tails[:, 4 * c:4 * c + 3], xcr[:, 512:515])
                for kk in range(4):
                    ts(cvd[:, dsl + 128 * kk:dsl + 128 * kk + 128], c_idb[:],
                       cw_sb[:, 4 * c + kk:4 * c + kk + 1], None, OP.mult)
                conv_fence[c] = dve_fence()
                bank_release(bm, sdv)
                if c >= 1:
                    emit_conv_chunk(c - 1)
            else:
                if m == CHI:
                    emit_conv_chunk(CHI - 1)
                czz = m - CHI
                zsl = czz % 2
                if szsl_free[zsl] is not None:
                    s, v = szsl_free[zsl]
                    B.wt("act", s, v)
                B.wt("act", spe)
                act(szsl[:, 512 * zsl:512 * zsl + 512], pb[bm][:, 0:512], AF.Silu, sem=sac)
                bank_release(bm, sac)
                B.wt("pl", sac)
                B.dma("pl", dzo, t_zsp[128 * czz:128 * czz + 128, tc0:tc0 + 512],
                      szsl[:, 512 * zsl:512 * zsl + 512])
                szsl_free[zsl] = (dzo, dzo.n)
                zsp_written[(t, czz)] = dzo.n
        ln_rows(0, 1, mu_i=0, rstd_i=2, invn_c=C_2048, tmp_i=1)
        B.wt("dve", sac)
        ts(rowap(4), rowap(0), -1.0, None, OP.mult)
        cp("dve", negmur_r, rowap(4), sem=sdv)
        n_nmr = dve_fence()
        B.wt("pe", sfen, n_nmr)
        mm(pb[7][0:80, 0:512], cg_r, negmur_r, False, True, sem=spe)
        n_xp = spe.n
        bcast_rows(rowap(0), 2)
        bcast_rows(rowap(2), 3)
        n_bc2 = spe.n
        B.wt("dve", spe, n_xp)
        cp("dve", xpsb[0:80, 0:512], pb[7][0:80, 0:512])
        B.wt("dve", spe, n_bc2)
        tt(xpsb[0:80, 0:512], xpsb[0:80, 0:512], pb[3][0:80, 0:512], OP.mult)
        ts(xpsb[0:80, 0:512], xpsb[0:80, 0:512], cb_col, None, OP.add, sem=sdv)
        bank_release(7, sdv)
        cp("dve", sfm[0:16, 0:512], xpsb[32:48, 0:512])
        cp("dve", sfm[0:16, 512:1024], xpsb[64:80, 0:512])
        tt(usb[0:16, 0:512], sfm[0:16, 0:512], sfm[0:16, 512:1024], OP.mult, sem=sdv)
        n_u = dve_fence()
        B.wt("act", sfen, n_u)
        act(rowap(5), xpsb[0:1, 0:512], AF.Abs, sem=sac)
        B.wt("act", sac)
        act(rowap(5), rowap(5), AF.Exp, scale=cst(C_NEG1), sem=sac)
        B.wt("act", sac)
        act(rowap(5), rowap(5), AF.Ln, bias=cst(C_ONE), sem=sac)
        B.wt("act", sac)
        act(rowap(1), xpsb[0:1, 0:512], AF.Relu, sem=sac)
        B.wt("dve", sac)
        tt(rowap(5), rowap(5), rowap(1), OP.add)
        ts(rowap(5), rowap(5), 1e-6, 10.0, OP.max, OP.min, sem=sdv)
        n_delta = dve_fence()
        bank_acquire(4, "pe")
        B.wt("pe", sfen)
        for tcg in range(NTC):
            tp(pb[4][:, tcg:tcg + 1], rowap(5)[0:1, 128 * tcg:128 * tcg + 128],
               c_idf[0:1, 0:1])
        for tcg in range(NTC):
            tp(pb[4][:, 16 + 16 * tcg:16 + 16 * tcg + 16],
               usb[0:16, 128 * tcg:128 * tcg + 128], c_idf[0:16, 0:16],
               sem=spe if tcg == NTC - 1 else None)
        n_tr = spe.n
        B.wt("dve", spe, n_tr)
        cp("dve", dcol[:, 0:4], pb[4][:, 0:4], sem=sdv)
        cp("dve", uT[:, 0:64], pb[4][:, 16:80], sem=sdv)
        bank_release(4, sdv)
        for tcg in range(NTC):
            for n in range(NN):
                idx = tcg * NN + n
                ts(diag[:, 128 * idx:128 * idx + 128], c_idb[:],
                   uT[:, idx:idx + 1], None, OP.mult,
                   sem=sdv if idx == NTC * NN - 1 else None)
        n_diag = dve_fence()
        if DBG:
            B.wt("pl", sdv, n_diag)
            for c in range(CHI):
                B.dma("pl", g("ddb"), t_dbg1[128 * c:128 * c + 128, tc0:tc0 + 512],
                      xcs[:, 512 * c:512 * c + 512])
            B.dma("pl", g("ddb"), t_dbg3[0:1, tc0:tc0 + 512], rowap(5))
            B.dma("pl", g("ddb"), t_dbg4[0:16, tc0:tc0 + 512], usb[0:16, 0:512])
            B.wt("pl", g("ddb"))

        def load_abc(qq):
            asl = qq % 2
            if abc_free[asl] is not None:
                s, v = abc_free[asl]
                B.wt("sp", s, v)
                B.wt("pl", s, v)
            B.wt("sp", dwb, n_abig)
            B.wt("pl", dwb, n_abig)
            srcq = t_abig.rearrange("(p n i) -> p n i", p=128, n=NN)[:, :, QW * qq:QW * qq + QW]
            dst = abc[:, 4096 * asl:4096 * asl + 4096].rearrange("p (n i) -> p n i", n=NN)
            B.dma("sp", dab, dst[:, 0:8, :], srcq[:, 0:8, :])
            B.dma("pl", dab, dst[:, 8:16, :], srcq[:, 8:16, :])
            return dab.n

        abc_ready = {0: load_abc(0)}
        for q in range(NQ):
            asl = q % 2
            dab_n = abc_ready[q]
            if q + 1 < NQ:
                abc_ready[q + 1] = load_abc(q + 1)
            for tcg in range(NTC):
                bS = 4 + (tcg % 2)
                bank_acquire(bS, "pe")
                for h2 in range(2):
                    B.wt("act", g("dab"), dab_n)
                    B.wt("act", sfen, n_diag)
                    if ebf_busy[h2] is not None:
                        s, v = ebf_busy[h2]
                        B.wt("act", s, v)
                    act(ebf[:, 2048 * h2:2048 * h2 + 2048],
                        abc[:, 4096 * asl + 2048 * h2:4096 * asl + 2048 * h2 + 2048],
                        AF.Exp, scale=dcol[:, tcg:tcg + 1], sem=sac)
                    B.wt("dve", sac)
                    ts(ebf[:, 2048 * h2:2048 * h2 + 2048],
                       ebf[:, 2048 * h2:2048 * h2 + 2048], 1e-6, 1.0,
                       OP.max, OP.min, sem=sdv)
                    B.wt("pe", sdv)
                    for n8 in range(8):
                        n = 8 * h2 + n8
                        idx = tcg * NN + n
                        mm(pb[bS][:, 0:QW], diag[:, 128 * idx:128 * idx + 128],
                           ebf[:, QW * n:QW * n + QW], n == 0, False,
                           sem=spe if n8 == 7 and h2 == 0 else None)
                    if h2 == 1:
                        mm(pb[bS][:, 0:QW], c_idb[:], dbc_sb[:, QW * q:QW * q + QW],
                           False, True, sem=spe)
                    ebf_busy[h2] = (spe, spe.n)
                if tcg == NTC - 1:
                    abc_free[asl] = (sac, sac.n)
                B.wt("dve", spe)
                ssl = tcg % 2
                cp("dve", stm[:, 0:256], pb[bS][:, 0:QW], sem=sdv, drain=False)
                dve_fence()
                bank_release(bS, sdv)
                bT = 6
                bank_acquire(bT, "pe")
                B.wt("pe", sfen)
                tp(pb[bT][:, 0:128], stm[:, 0:128], c_idf[:])
                tp(pb[bT][:, 128:256], stm[:, 128:256], c_idf[:], sem=spe)
                B.wt("dve", spe)
                cp("dve", sfm[:, 128 * tcg:128 * tcg + 128], pb[bT][:, 0:128])
                cp("dve", sfm[:, 512 + 128 * tcg:512 + 128 * tcg + 128],
                   pb[bT][:, 128:256], sem=sdv, drain=False)
                bank_release(bT, sdv)
            for half in range(2):
                c = 2 * q + half
                zsl = c % 2
                if szsl_free[zsl] is not None:
                    s, v = szsl_free[zsl]
                    B.wt("sp", s, v)
                B.wt("sp", dzo, zsp_written[(t, c)])
                B.dma("sp", dzi, szsl[:, 512 * zsl:512 * zsl + 512],
                      t_zsp[128 * c:128 * c + 128, tc0:tc0 + 512])
                B.wt("dve", dzi)
                B.wt("dve", spe)
                tt(tmp1[:], xcs[:, 512 * c:512 * c + 512], pb[2][:, 0:512], OP.subtract)
                tt(tmp1[:], tmp1[:], pb[3][:, 0:512], OP.mult)
                ts(tmp1[:], tmp1[:], gam_sb[:, c:c + 1], bet_sb[:, c:c + 1],
                   OP.mult, OP.add)
                tt(tmp1[:], tmp1[:], sfm[:, 512 * half:512 * half + 512], OP.mult)
                tt(gbf[:, 512 * c:512 * c + 512], tmp1[:],
                   szsl[:, 512 * zsl:512 * zsl + 512], OP.mult, sem=sdv, drain=False)
                szsl_free[zsl] = (sdv, sdv.n)
                if DBG:
                    B.wt("pl", sdv)
                    B.dma("pl", g("ddb"), t_dbg2[128 * c:128 * c + 128, tc0:tc0 + 512],
                          sfm[:, 512 * half:512 * half + 512])
                    B.wt("pl", g("ddb"))
        n_g = dve_fence()
        bank_release(2, sdv); bank_release(3, sdv)

        bank_acquire(0, "pe"); bank_acquire(1, "pe")
        B.wt("sp", dwb, n_wcast)
        for m in range(CHH):
            sl = m % 2
            if wout_free[sl] is not None:
                s, v = wout_free[sl]
                B.wt("sp", s, v)
            B.dma("sp", dwo, woutb[:, 2048 * sl:2048 * sl + 2048],
                  t_woutb.rearrange("(k p) m -> p k m", p=128)[:, :, 128 * m:128 * m + 128])
            bm = 4 + (m % 2)
            bank_acquire(bm, "pe")
            B.wt("pe", dwo)
            B.wt("pe", sfen, n_g)
            for kk in range(CHI):
                mm(pb[bm][:, 0:512],
                   woutb[:, 2048 * sl + 128 * kk:2048 * sl + 128 * kk + 128],
                   gbf[:, 512 * kk:512 * kk + 512], kk == 0, kk == CHI - 1,
                   sem=spe if kk == CHI - 1 else None)
            wout_free[sl] = (spe, spe.n)
            k, dn = load_x(128 * m, tc0)
            B.wt("dve", dxa, dn)
            B.wt("dve", spe)
            tt(x_pool[:, 512 * k:512 * k + 512], pb[bm][:, 0:512],
               x_pool[:, 512 * k:512 * k + 512], OP.add, sem=sdv)
            bank_release(bm, sdv)
            B.wt("sp", sdv)
            B.dma("sp", dxo, t_xmid[128 * m:128 * m + 128, tc0:tc0 + 512],
                  x_pool[:, 512 * k:512 * k + 512])
            B.wt("dve", spe)
            ln_stats_chunk(x_pool[:, 512 * k:512 * k + 512], m % 2, 0, 1,
                           m == 0, m == CHH - 1)
            xslot_free[k] = [(dxo, dxo.n), (sdv, sdv.n)]
        ln_rows(0, 1, mu_i=0, rstd_i=2, invn_c=C_1024, tmp_i=1)
        B.wt("dve", sac)
        tt(rowap(4), rowap(0), rowap(2), OP.mult)
        ts(rowap(4), rowap(4), -1.0, None, OP.mult, sem=sdv)
        B.wt("sp", sdv)
        B.dma("sp", dst, t_stash[0:1, tc0:tc0 + 512], rowap(2))
        B.dma("sp", dst, t_stash[1:2, tc0:tc0 + 512], rowap(4))
        bcast_rows(rowap(2), 4)
        bcast_rows(rowap(4), 5)
        n_bc1 = spe.n
        for m in range(CHH):
            B.wt("sp", dxo)
            k, dn = load_x(128 * m, tc0, src=t_xmid, dsem=dxi)
            if DBG:
                B.wt("pl", dxi, dn)
                B.dma("pl", g("ddb"), t_dbg6[128 * m:128 * m + 128, tc0:tc0 + 512],
                      x_pool[:, 512 * k:512 * k + 512])
            B.wt("dve", dxi, dn)
            B.wt("dve", spe, n_bc1)
            tt(tmp1[:], x_pool[:, 512 * k:512 * k + 512], pb[4][:, 0:512], OP.mult)
            tt(tmp1[:], tmp1[:], pb[5][:, 0:512], OP.add, sem=sdv)
            B.wt("dve", sdv)
            red(mx1c[:, 8 * t + m:8 * t + m + 1], tmp1[:], OP.max, absval=True, sem=sdv)
            xslot_free[k] = [(sdv, sdv.n)]
        bank_release(4, sdv); bank_release(5, sdv)

    dcc, cc1s = g("dcc"), g("cc1")
    B.wt("dve", sdv)
    red(mxp[:, 0:1], mx1c[:], OP.max, absval=False, sem=sdv)
    n_mx = dve_fence()
    bank_acquire(6, "pe")
    B.wt("pe", sfen, n_mx)
    tp(pb[6][0:1, 0:128], mxp[:, 0:1], c_idf[:], sem=spe)
    B.wt("dve", spe)
    cp("dve", rowap(3)[0:1, 0:128], pb[6][0:1, 0:128], sem=sdv)
    bank_release(6, sdv)
    B.wt("dve", sdv)
    red(rowap(3)[0:1, 128:129], rowap(3)[0:1, 0:128], OP.max, absval=False, sem=sdv)
    B.wt("sp", sdv)
    B.dma("sp", dcc, cc1_in, rowap(3)[0:1, 128:129])
    B.wt("pl", dcc)
    cc("AllReduce", OP.max, [cc1_in], [cc1_out], cc1s)
    B.wt("sp", cc1s, 1)
    B.dma("sp", dcc, rowap(3)[0:1, 129:130], cc1_out)
    B.wt("dve", dcc)
    tt(rowap(3)[0:1, 130:131], rowap(3)[0:1, 129:130], scal[0:1, 2:3], OP.mult)
    ts(rowap(3)[0:1, 130:131], rowap(3)[0:1, 130:131], cst(C_R127), None, OP.mult, sem=sdv)
    B.wt("dve", sdv)
    mset("dve", rowap(3)[0:1, 136:144], 1.0, sem=sdv)
    B.wt("dve", sdv)
    cp("dve", rowap(3)[0:1, 136:137], rowap(3)[0:1, 130:131], sem=sdv)
    B.wt("dve", sdv)
    B.op("dve", lambda E: E.reciprocal(rowap(3)[0:1, 144:152], rowap(3)[0:1, 136:144]), sem=sdv)
    n_rx = dve_fence()
    bank_acquire(6, "pe")
    B.wt("pe", sfen, n_rx)
    mm(pb[6][:, 0:1], c_ones1[:], rowap(3)[0:1, 130:131], True, True)
    mm(pb[6][:, 1:2], c_ones1[:], rowap(3)[0:1, 144:145], True, True, sem=spe)
    B.wt("dve", spe)
    cp("dve", scal[:, 4:5], pb[6][:, 0:1], sem=sdv)
    cp("dve", scal[:, 5:6], pb[6][:, 1:2], sem=sdv)
    bank_release(6, sdv)
    n_sx = sdv.n

    for snm in ("dxa", "dwi", "dwo", "dab", "dzo", "dzi", "dxo", "dxi", "dst"):
        for e in ENGS:
            B.wt(e, g(snm))
    esB.close()
    esC = ExitStack()
    sbC = lambda name, shape, dt: esC.enter_context(nc.sbuf_tensor(name, shape, dt))
    xq = sbC("xq", [128, CHH * 512], BF16)
    qv = sbC("qv", [128, 512], F32)
    qi = sbC("qi", [128, 512], I32)
    qf = sbC("qf", [128, 512], F32)
    qg = sbC("qg", [128, 512], F32)
    glb = sbC("glb", [128, 2 * CHH * 128], BF16)
    ulb = sbC("ulb", [128, 2 * CHH * 128], BF16)
    sgt = sbC("sgt", [128, 512], F32)
    hbf = sbC("hbf", [128, 2 * 512], F32)
    mxh = sbC("mxh", [128, 24], F32)
    hqt = sbC("hqt", [128, CHF * 512], BF16)
    hin = sbC("hin", [128, 2 * 512], F32)
    dlb = sbC("dlb", [128, 2 * CHF * 128], BF16)
    yst = sbC("yst", [128, 2 * 512], F32)
    xm_pool = sbC("xm_pool", [128, 2 * 512], F32)
    ln1r = sbC("ln1r", [1, 2 * 512], F32)

    dho, dhi, dgl2, dy = g("dho"), g("dhi"), g("dgl"), g("dy")
    emit_prep(len(prep_units))
    n_prep_dgl = dgl.n
    mset("dve", mxh[:], 0.0)
    mset("dve", mxp[:, 1:2], 0.0, sem=sdv)
    B.wt("dve", sdv)

    gl_free = [None, None]
    ul_free = [None, None]
    hb_free = [None, None]
    xm_free = [None, None]
    hin_free = [None, None]
    dl_free = [None, None]
    yst_free = [None, None]

    for t in range(NT):
        tc0 = t * T
        B.dma("sp", dst, ln1r[0:1, 0:512], t_stash[0:1, tc0:tc0 + 512])
        B.dma("sp", dst, ln1r[0:1, 512:1024], t_stash[1:2, tc0:tc0 + 512])
        nst = dst.n
        bank_acquire(4, "pe"); bank_acquire(5, "pe")
        B.wt("pe", dst, nst)
        mm(pb[4][:, 0:512], c_ones1[:], ln1r[0:1, 0:512], True, True)
        mm(pb[5][:, 0:512], c_ones1[:], ln1r[0:1, 512:1024], True, True, sem=spe)
        n_bb = spe.n
        for m in range(CHH):
            sl = m % 2
            if xm_free[sl] is not None:
                s, v = xm_free[sl]
                B.wt("sp", s, v)
            B.dma("sp", dxi, xm_pool[:, 512 * sl:512 * sl + 512],
                  t_xmid[128 * m:128 * m + 128, tc0:tc0 + 512])
            B.wt("dve", dxi)
            B.wt("dve", spe, n_bb)
            B.wt("dve", sdv, n_sx)
            tt(qv[:], xm_pool[:, 512 * sl:512 * sl + 512], pb[4][:, 0:512], OP.mult)
            tt(qv[:], qv[:], pb[5][:, 0:512], OP.add, sem=sdv, drain=False)
            xm_free[sl] = (sdv, sdv.n)
            ts(qv[:], qv[:], scal[:, 5:6], cstp(C_HALF), OP.mult, OP.add)
            cp("dve", qi[:], qv[:])
            cp("dve", qf[:], qi[:])
            tt(qg[:], qf[:], qv[:], OP.is_gt)
            tt(qf[:], qf[:], qg[:], OP.subtract)
            ts(qf[:], qf[:], cstp(C_N128), 127.0, OP.max, OP.min)
            ts(xq[:, 512 * m:512 * m + 512], qf[:], scal[:, 4:5], None, OP.mult)
        n_xq = dve_fence()
        bank_release(4, sdv); bank_release(5, sdv)
        for m in range(CHF):
            sl = m % 2
            for (buf, free, dram) in ((glb, gl_free, t_glhs), (ulb, ul_free, t_ulhs)):
                if free[sl] is not None:
                    s, v = free[sl]
                    B.wt("pl", s, v)
                B.wt("pl", dgl, n_prep_dgl)
                B.dma("pl", dgl2, buf[:, 1024 * sl:1024 * sl + 1024],
                      dram.rearrange("k p m -> p k m")[:, :, 128 * m:128 * m + 128])
            ndgl = dgl2.n
            bg = 4 + (m % 2)
            bu = 6 + (m % 2)
            bank_acquire(bg, "pe"); bank_acquire(bu, "pe")
            B.wt("pe", dgl2, ndgl)
            B.wt("pe", sfen, n_xq)
            for kk in range(CHH):
                mm(pb[bg][:, 0:512], glb[:, 1024 * sl + 128 * kk:1024 * sl + 128 * kk + 128],
                   xq[:, 512 * kk:512 * kk + 512], kk == 0, kk == CHH - 1)
            for kk in range(CHH):
                mm(pb[bu][:, 0:512], ulb[:, 1024 * sl + 128 * kk:1024 * sl + 128 * kk + 128],
                   xq[:, 512 * kk:512 * kk + 512], kk == 0, kk == CHH - 1,
                   sem=spe if kk == CHH - 1 else None)
            gl_free[sl] = (spe, spe.n)
            ul_free[sl] = (spe, spe.n)
            B.wt("act", spe)
            B.wt("act", sdv)
            act(sgt[:], pb[bg][:, 0:512], AF.Silu, sem=sac)
            bank_release(bg, sac)
            hsl = m % 2
            if hb_free[hsl] is not None:
                s, v = hb_free[hsl]
                B.wt("dve", s, v)
            B.wt("dve", sac)
            tt(hbf[:, 512 * hsl:512 * hsl + 512], sgt[:], pb[bu][:, 0:512],
               OP.mult, sem=sdv)
            bank_release(bu, sdv)
            B.wt("dve", sdv)
            red(mxh[:, m:m + 1], hbf[:, 512 * hsl:512 * hsl + 512], OP.max,
                absval=True, sem=sdv)
            B.wt("pl", sdv)
            B.dma("pl", dho, t_hd[128 * m:128 * m + 128, tc0:tc0 + 512],
                  hbf[:, 512 * hsl:512 * hsl + 512])
            hb_free[hsl] = (dho, dho.n)
        B.wt("dve", sdv)
        red(mxp[:, 2:3], mxh[:], OP.max, absval=False, sem=sdv)
        B.wt("dve", sdv)
        tt(mxp[:, 1:2], mxp[:, 1:2], mxp[:, 2:3], OP.max, sem=sdv)
    dve_fence()

    cc2s = g("cc2")
    bank_acquire(6, "pe")
    B.wt("pe", sdv)
    B.wt("pe", sfen)
    tp(pb[6][0:1, 0:128], mxp[:, 1:2], c_idf[:], sem=spe)
    B.wt("dve", spe)
    cp("dve", ln1r[0:1, 0:128], pb[6][0:1, 0:128], sem=sdv)
    bank_release(6, sdv)
    B.wt("dve", sdv)
    red(ln1r[0:1, 128:129], ln1r[0:1, 0:128], OP.max, absval=False, sem=sdv)
    B.wt("sp", sdv)
    B.dma("sp", dcc, cc2_in, ln1r[0:1, 128:129])
    B.wt("pl", dcc)
    cc("AllReduce", OP.max, [cc2_in], [cc2_out], cc2s)
    B.wt("sp", cc2s, 1)
    B.dma("sp", dcc, ln1r[0:1, 129:130], cc2_out)
    B.wt("dve", dcc)
    tt(ln1r[0:1, 130:131], ln1r[0:1, 129:130], scal[0:1, 3:4], OP.mult)
    ts(ln1r[0:1, 130:131], ln1r[0:1, 130:131], cst(C_R127), None, OP.mult, sem=sdv)
    B.wt("dve", sdv)
    mset("dve", ln1r[0:1, 136:144], 1.0, sem=sdv)
    B.wt("dve", sdv)
    cp("dve", ln1r[0:1, 136:137], ln1r[0:1, 130:131], sem=sdv)
    B.wt("dve", sdv)
    B.op("dve", lambda E: E.reciprocal(ln1r[0:1, 144:152], ln1r[0:1, 136:144]), sem=sdv)
    n_rh = dve_fence()
    bank_acquire(6, "pe")
    B.wt("pe", sfen, n_rh)
    mm(pb[6][:, 0:1], c_ones1[:], ln1r[0:1, 130:131], True, True)
    mm(pb[6][:, 1:2], c_ones1[:], ln1r[0:1, 144:145], True, True, sem=spe)
    B.wt("dve", spe)
    cp("dve", scal[:, 6:7], pb[6][:, 0:1], sem=sdv)
    cp("dve", scal[:, 7:8], pb[6][:, 1:2], sem=sdv)
    bank_release(6, sdv)
    n_sh = sdv.n
    if DBG:
        B.wt("pl", sdv, n_sh)
        B.dma("pl", g("ddb"), t_dbg7, scal[:])
        B.wt("pl", g("ddb"))

    for t in range(NT):
        tc0 = t * T
        for kk in range(CHF):
            sl = kk % 2
            if hin_free[sl] is not None:
                s, v = hin_free[sl]
                B.wt("sp", s, v)
            B.wt("sp", dho)
            B.dma("sp", dhi, hin[:, 512 * sl:512 * sl + 512],
                  t_hd[128 * kk:128 * kk + 128, tc0:tc0 + 512])
            B.wt("dve", dhi)
            B.wt("dve", sdv, n_sh)
            ts(qv[:], hin[:, 512 * sl:512 * sl + 512], scal[:, 7:8], cstp(C_HALF),
               OP.mult, OP.add, sem=sdv, drain=False)
            hin_free[sl] = (sdv, sdv.n)
            cp("dve", qi[:], qv[:])
            cp("dve", qf[:], qi[:])
            tt(qg[:], qf[:], qv[:], OP.is_gt)
            tt(qf[:], qf[:], qg[:], OP.subtract)
            ts(hqt[:, 512 * kk:512 * kk + 512], qf[:], cstp(C_N128), 127.0,
               OP.max, OP.min)
        n_hq = dve_fence()
        for m in range(CHH):
            sl = m % 2
            if dl_free[sl] is not None:
                s, v = dl_free[sl]
                B.wt("pl", s, v)
            B.wt("pl", dgl, n_prep_dgl)
            B.dma("pl", dgl2, dlb[:, 2816 * sl:2816 * sl + 2816],
                  t_dlhs.rearrange("k p m -> p k m")[:, :, 128 * m:128 * m + 128])
            ndl = dgl2.n
            bm = 4 + (m % 2)
            bank_acquire(bm, "pe")
            B.wt("pe", dgl2, ndl)
            B.wt("pe", sfen, n_hq)
            for kk in range(CHF):
                mm(pb[bm][:, 0:512],
                   dlb[:, 2816 * sl + 128 * kk:2816 * sl + 128 * kk + 128],
                   hqt[:, 512 * kk:512 * kk + 512], kk == 0, kk == CHF - 1,
                   sem=spe if kk == CHF - 1 else None)
            dl_free[sl] = (spe, spe.n)
            ysl = m % 2
            if yst_free[ysl] is not None:
                s, v = yst_free[ysl]
                B.wt("sp", s, v)
            if xm_free[ysl] is not None:
                s, v = xm_free[ysl]
                B.wt("sp", s, v)
            B.dma("sp", dxi, xm_pool[:, 512 * ysl:512 * ysl + 512],
                  t_xmid[128 * m:128 * m + 128, tc0:tc0 + 512])
            B.wt("dve", dxi)
            B.wt("dve", spe)
            stt(yst[:, 512 * ysl:512 * ysl + 512], pb[bm][:, 0:512],
                scal[:, 6:7], xm_pool[:, 512 * ysl:512 * ysl + 512],
                OP.mult, OP.add, sem=sdv)
            bank_release(bm, sdv)
            xm_free[ysl] = (sdv, sdv.n)
            B.wt("sp", sdv)
            B.dma("sp", dy, d_y[128 * m:128 * m + 128, tc0:tc0 + 512],
                  yst[:, 512 * ysl:512 * ysl + 512])
            yst_free[ysl] = (dy, dy.n)
    B.wt("sp", dy, dy.n)
    B.wt("pl", dy, dy.n)

    with nc.Block() as block:
        @block.sync
        def _(E):
            for f in B.em["sp"]:
                f(E)

        @block.tensor
        def _(E):
            for f in B.em["pe"]:
                f(E)

        @block.vector
        def _(E):
            for f in B.em["dve"]:
                f(E)

        @block.scalar
        def _(E):
            for f in B.em["act"]:
                f(E)

        @block.gpsimd
        def _(E):
            for f in B.em["pl"]:
                f(E)

    esC.close()
    es.close()
    return nc


def kernel(L0_attn_x, L0_attn_W_in, L0_attn_conv_w, L0_attn_ln_gamma,
           L0_attn_ln_beta, L0_attn_W_xproj, L0_attn_A_param, L0_attn_D,
           L0_attn_W_out, L0_ffn_clip_x, L0_ffn_clip_down, L0_ffn_w_gate,
           L0_ffn_w_up, L0_ffn_w_down):
    xf = np.asarray(L0_attn_x, np.float32)
    w_in = np.ascontiguousarray(np.asarray(L0_attn_W_in, np.float32))
    conv_w = np.asarray(L0_attn_conv_w, np.float32).reshape(CI, 4)
    conv_t = np.ascontiguousarray(conv_w.reshape(CHI, 128, 4).transpose(1, 0, 2).reshape(128, CHI * 4))
    gam_t = np.ascontiguousarray(np.asarray(L0_attn_ln_gamma, np.float32).reshape(CHI, 128).T)
    bet_t = np.ascontiguousarray(np.asarray(L0_attn_ln_beta, np.float32).reshape(CHI, 128).T)
    wxp = np.asarray(L0_attn_W_xproj, np.float32)
    wxp_pad = np.zeros((CI, 80), np.float32)
    wxp_pad[:, 0:1] = wxp[:, 0:1]
    wxp_pad[:, 32:48] = wxp[:, 1:17]
    wxp_pad[:, 64:80] = wxp[:, 17:33]
    wxp_t = np.ascontiguousarray(wxp_pad.reshape(CHI, 128, 80).transpose(1, 0, 2).reshape(128, CHI * 80))
    a_t = np.ascontiguousarray(np.asarray(L0_attn_A_param, np.float32).T)
    d_row = np.ascontiguousarray(np.asarray(L0_attn_D, np.float32).reshape(1, CI))
    w_out = np.ascontiguousarray(np.asarray(L0_attn_W_out, np.float32))
    w_gate = np.ascontiguousarray(np.asarray(L0_ffn_w_gate, np.float32))
    w_up = np.ascontiguousarray(np.asarray(L0_ffn_w_up, np.float32))
    w_down = np.ascontiguousarray(np.asarray(L0_ffn_w_down, np.float32))
    cx = np.asarray(L0_ffn_clip_x, np.float32).reshape(1, 1)
    cd = np.asarray(L0_ffn_clip_down, np.float32).reshape(1, 1)
    idf = np.eye(128, dtype=np.float32)
    idb = idf.astype(ml_dtypes.bfloat16)
    ones1 = np.ones((1, 128), np.float32)
    consts = np.broadcast_to(np.asarray(CONSTVALS, np.float32), (128, len(CONSTVALS))).copy()

    nc = build_program()
    in_maps = []
    for b in range(8):
        m = {
            "x": np.ascontiguousarray(xf[b].T),
            "w_in": w_in, "conv_w": conv_t, "gamma_t": gam_t, "beta_t": bet_t,
            "d_row": d_row, "wxp_t": wxp_t, "a_t": a_t, "w_out": w_out,
            "clip_x": cx, "clip_dn": cd,
            "w_gate": w_gate, "w_up": w_up, "w_down": w_down,
            "idf": idf, "idb": idb, "ones1": ones1,
            "consts": consts,
        }
        in_maps.append(m)
    res = run_bass_kernel_spmd(nc, in_maps, core_ids=list(range(8)))
    out = np.stack([np.asarray(res.results[b]["y"], np.float32).T for b in range(8)])
    return out
